# version 29
# baseline (speedup 1.0000x reference)
import contextlib
import os
import numpy as np

os.environ.setdefault("TRN_ALLOW_ICI_ALL_TO_ALL", "1")

import concourse.bass as bass
import concourse.mybir as mybir
from concourse.bass_utils import run_bass_kernel_spmd

F32 = mybir.dt.float32
BF16 = mybir.dt.bfloat16
AF = mybir.ActivationFunctionType
ALU = mybir.AluOpType
AX = mybir.AxisListType

NCORES = 8
T = 1024
D = 2048
DI = 4096
NST = 16
RNK = 128
HF = 8192
S = 512
NT = 4
W = 256
NW = 8192 // W
EPS = 1e-5
RG = [list(range(NCORES))]


def build_program():
    nc = bass.Bass("TRN2", target_bir_lowering=False, debug=False)

    def ein(name, shape, dt=F32):
        return nc.dram_tensor(name, shape, dt, kind="ExternalInput").ap()

    h_main = ein("h_main", [T, D])
    h_halo = ein("h_halo", [3, D])
    winT = ein("winT", [D, HF])
    woutT = ein("woutT", [DI, D])
    wgT = ein("wgT", [D, HF])
    wuT = ein("wuT", [D, HF])
    wdnT = ein("wdnT", [HF, D])
    wx_in = ein("wx_in", [128, 32 * 160])
    wdt_in = ein("wdt_in", [128, DI])
    a_sl = ein("a_sl", [128, NT * NST])
    d_sl = ein("d_sl", [128, NT])
    convw = ein("convw", [128, 32 * 4])
    convb = ein("convb", [128, 32])
    bdt = ein("bdt", [128, 32])
    bout = ein("bout", [1, D])
    ones1 = ein("ones1", [1, 128])
    ident = ein("ident", [128, 128])
    out_o = nc.dram_tensor("o", [T, D], F32, kind="ExternalOutput").ap()

    def idr(name, shape, dt=F32):
        return nc.dram_tensor(name, shape, dt)

    def idrs(name, shape, dt=F32):
        return nc.dram_tensor(name, shape, dt, addr_space="Shared")

    wxTb = idr("wxTb", [128, 32 * 160], BF16)
    wdtTb = idr("wdtTb", [128, DI], BF16)
    winTb = idr("winTb", [D, HF], BF16)
    wgTb = idr("wgTb", [D, HF], BF16)
    wuTb = idr("wuTb", [D, HF], BF16)
    woutTc = idr("woutTc", [DI, D], BF16)
    wdnTc = idr("wdnTc", [HF, D], BF16)
    z_buf = idr("z_buf", [DI, T])
    g_buf = idr("g_buf", [HF, T], BF16)
    u_buf = idr("u_buf", [HF, T], BF16)
    ffn_buf = idr("ffn_buf", [T, D])
    du_in = idr("du_in", [2 * DI, T])
    du_out = idr("du_out", [2 * DI, T])
    bc_in = idr("bc_in", [256, 128])
    bc_all = idr("bc_all", [NCORES * 256, 128])
    y_in = idr("y_in", [DI, T])
    y_loc = idr("y_loc", [DI, T])
    fence_in = idr("fence_in", [1, 8])
    fence_out = [idr(f"fence_out{i}", [NCORES, 8]) for i in range(2)]

    with contextlib.ExitStack() as ctx:
        def sbt(name, shape, dt=F32):
            return ctx.enter_context(nc.sbuf_tensor(name, shape, dt))

        sbA = sbt("sbA", [128, 64])
        sbAe = sbt("sbAe", [128, 64])
        sbAn = sbt("sbAn", [128, 64])
        sbD = sbt("sbD", [128, NT])
        sbCW = sbt("sbCW", [128, 128])
        sbCB = sbt("sbCB", [128, 32])
        sbBDT = sbt("sbBDT", [128, 32])
        sbONES = sbt("sbONES", [1, 128])
        sbID = sbt("sbID", [128, 128])
        sbSTAT = sbt("sbSTAT", [128, 16])

        AFSZ = 52400
        arena = sbt("arena", [128, AFSZ])
        arv = arena[:]

        def rf(off, size):
            assert off + size <= AFSZ, (off, size)
            return arv[:, off:off + size]

        def rbv(off, size_f32):
            return rf(off, size_f32).bitcast(BF16)

        XNT = rbv(0, 8216)

        def xnt(dk):
            return XNT[:, dk * 1027:(dk + 1) * 1027]

        WTB = [rbv(8216, 2048), rbv(10264, 2048)]
        RA = 12320

        A_xn = rf(RA + 0, 2048)
        A_h = rf(RA + 2048, 2048)
        A_scr = rf(RA + 4096, 2048)
        A_xt = [rf(RA + 6144, 1028), rf(RA + 7172, 1028)]
        A_hst = rf(RA + 8200, 512)
        A_trs = rf(RA + 8712, 32)
        A_bc = rf(RA + 8744, 1024)
        A_ct = rf(RA + 9768, 128)
        A_dl = [rf(RA + 9896, 1024), rf(RA + 10920, 1024)]
        A_sp = [rf(RA + 11944 + i * 512, 512) for i in range(3)]
        A_cacc = [rf(RA + 13480, 1024), rf(RA + 14504, 1024)]
        A_xcf = [rf(RA + 15528, 1024), rf(RA + 16552, 1024)]
        A_dtp = rbv(RA + 17576, 512)
        A_wxT = rbv(RA + 18088, 2560)
        A_wdtT = rbv(RA + 20648, 2048)
        A_xcb = rbv(RA + 22696, 16384)

        S_s = [rf(RA + 0, 4096), rf(RA + 0, 4096)]
        S_a = [rf(RA + 8192, 4096), rf(RA + 4096, 4096)]
        S_b = rf(RA + 12288, 4096)
        S_Br = rf(RA + 16384, 4096)
        S_Cr = rf(RA + 20480, 4096)
        S_dsl = [rf(RA + 24576, 1024), rf(RA + 25600, 1024)]
        S_usl = rf(RA + 26624, 1024)
        S_du = rf(RA + 27648, 1024)
        S_y = rf(RA + 28672, 1024)
        S_carry = rf(RA + 29696, 64)
        RFN = RA + 29760

        F_z = rf(RFN + 0, 1024)
        Fg = [[rbv(RFN + 4096 + w2 * 1024 + q * 256, 256) for q in range(4)]
              for w2 in range(2)]
        Fu = [[rbv(RFN + 6144 + w2 * 1024 + q * 256, 256) for q in range(4)]
              for w2 in range(2)]
        Fg_ld = [rbv(RFN + 0, 1024), rbv(RFN + 1024, 1024)]
        WTBg2 = [rbv(RFN + 0, 2048), rbv(RFN + 2048, 2048)]
        Fu_ld = [rbv(RFN + 2048, 1024), rbv(RFN + 3072, 1024)]
        F_glt = [rbv(RFN + 4096, 1024), rbv(RFN + 5120, 1024)]
        F_dstr = [rbv(RFN + 6144, 1024), rbv(RFN + 7168, 1024)]
        F_ffn = [rf(RFN + 8192 + i * 512, 512) for i in range(2)]

        P5_zs = [rf(RA + 0, 1024), rf(RA + 1024, 1024)]
        P5_y = [rf(RA + 2048, 1024), rf(RA + 3072, 1024)]
        P5_hb = rf(RA + 4096, 2048)
        P5_ffn = rf(RA + 6144, 2048)
        P5_brep = rf(RA + 8192, 2048)
        P5_out = [rf(RA + 10240, 2048), rf(RA + 12288, 2048)]
        P5_y2 = rbv(RA + 14336, 16384)
        P5_wo = [rbv(RA + 30720, 1024), rbv(RA + 31744, 1024)]

        P = [ctx.enter_context(nc.psum_tensor(f"ps{i}", [128, 512], F32))
             for i in range(8)]

        sems, cnt, waited = {}, {}, {}
        for s in ["sy", "gd", "ad", "cc", "pe", "act", "dve", "po"]:
            sems[s] = ctx.enter_context(nc.semaphore(s))
            cnt[s] = 0

        def inc(instr, s, k=1):
            instr.then_inc(sems[s], k)
            cnt[s] += k
            return cnt[s]

        def wt(eng, en, s, v):
            if not v or v <= 0:
                return
            wd = waited.setdefault(en, {})
            if wd.get(s, 0) >= v:
                return
            eng.wait_ge(sems[s], v)
            wd[s] = v

        block = ctx.enter_context(nc.Block())

        def sec(en, fn):
            getattr(block, en)(fn)

        tok = {}
        bank = [None] * 8

        def bank_wait(eng, en, i):
            if bank[i] is not None:
                wt(eng, en, bank[i][0], bank[i][1])

        def ph0_sy(sy):
            for dst, src in ((sbA, a_sl), (sbD, d_sl), (sbCW, convw),
                             (sbCB, convb), (sbBDT, bdt),
                             (sbONES, ones1), (sbID, ident)):
                inc(sy.dma_start(out=dst[:], in_=src), "sy", 16)
            tok["params"] = cnt["sy"]
        sec("sync", ph0_sy)

        def ph0_po(po):
            inc(po.dma_start(out=winTb.ap()[:, 0:DI], in_=winT[:, 0:DI]),
                "gd", 16)
            tok[("staged", "winx")] = cnt["gd"]
            wt(po, "po", "gd", cnt["gd"])
            for name, dst, src_ in (("wx", wxTb.ap(), wx_in),
                                    ("wdt", wdtTb.ap(), wdt_in),
                                    ("winz", winTb.ap()[:, DI:HF],
                                     winT[:, DI:HF])):
                inc(po.dma_start(out=dst, in_=src_), "gd", 16)
                tok[("staged", name)] = cnt["gd"]
                wt(po, "po", "gd", cnt["gd"])
            c0, c1 = 0, 2048
            inc(po.dma_start(out=wgTb.ap()[:, c0:c1],
                             in_=wgT[:, c0:c1]), "gd", 16)
            tok[("staged", "wg", 0)] = cnt["gd"]
            wt(po, "po", "gd", cnt["gd"])
            inc(po.dma_start(out=wuTb.ap()[:, c0:c1],
                             in_=wuT[:, c0:c1]), "gd", 16)
            tok[("staged", "wu", 0)] = cnt["gd"]
            wt(po, "po", "gd", cnt["gd"])
        sec("gpsimd", ph0_po)

        def ph0_sc_a(sc):
            wt(sc, "sc", "sy", tok["params"])
            inc(sc.activation(out=sbAe[:], in_=sbA[:], func=AF.Exp), "act")
            tok["a_exp"] = cnt["act"]
        sec("scalar", ph0_sc_a)

        def ph0_ve_a(ve):
            wt(ve, "ve", "act", tok["a_exp"])
            inc(ve.tensor_scalar_mul(sbAn[:], sbAe[:], -1.0), "dve")
            tok["a_neg"] = cnt["dve"]
        sec("vector", ph0_ve_a)

        for i in range(9):
            lt = i - 1
            if i == 0:
                def f_ms(ve):
                    inc(ve.memset(A_h[:], 0.0), "dve")
                    tok["memset_h"] = cnt["dve"]
                sec("vector", f_ms)

            def f_sy(sy, i=i, lt=lt):
                if i == 0:
                    wt(sy, "sy", "dve", tok["memset_h"])
                    inc(sy.dma_start(out=A_h[0:3, :], in_=h_halo), "sy", 16)
                else:
                    wt(sy, "sy", "dve", tok.get(("xn", i - 1), 0))
                    inc(sy.dma_start(out=A_h[:],
                                     in_=h_main[lt * 128:(lt + 1) * 128, :]),
                        "sy", 16)
                tok[("hload", i)] = cnt["sy"]
            sec("sync", f_sy)

            def f_sq(sc, i=i):
                wt(sc, "sc", "sy", tok[("hload", i)])
                wt(sc, "sc", "dve", tok.get(("lnstat", i - 1), 0))
                inc(sc.activation(out=A_scr[:], in_=A_h[:], func=AF.Square),
                    "act")
                tok[("hsq", i)] = cnt["act"]
            sec("scalar", f_sq)

            def f_ve(ve, i=i):
                wt(ve, "ve", "sy", tok[("hload", i)])
                wt(ve, "ve", "act", tok.get(("rstd", i - 1), 0))
                ve.tensor_reduce(out=sbSTAT[:, 0:1], in_=A_h[:], axis=AX.X,
                                 op=ALU.add)
                wt(ve, "ve", "act", tok[("hsq", i)])
                v0 = inc(ve.tensor_reduce(out=sbSTAT[:, 1:2], in_=A_scr[:],
                                          axis=AX.X, op=ALU.add), "dve")
                wt(ve, "ve", "dve", v0)
                v1 = inc(ve.tensor_scalar_mul(sbSTAT[:, 3:4], sbSTAT[:, 0:1],
                                              -1.0 / 2048.0), "dve")
                wt(ve, "ve", "dve", v1)
                v2 = inc(ve.tensor_mul(sbSTAT[:, 4:5], sbSTAT[:, 3:4],
                                       sbSTAT[:, 3:4]), "dve")
                v3 = inc(ve.tensor_scalar_mul(sbSTAT[:, 5:6], sbSTAT[:, 1:2],
                                              1.0 / 2048.0), "dve")
                wt(ve, "ve", "dve", v3)
                v4 = inc(ve.tensor_sub(sbSTAT[:, 6:7], sbSTAT[:, 5:6],
                                       sbSTAT[:, 4:5]), "dve")
                wt(ve, "ve", "dve", v4)
                inc(ve.tensor_scalar_add(sbSTAT[:, 6:7], sbSTAT[:, 6:7],
                                         EPS), "dve")
                tok[("lnstat", i)] = cnt["dve"]
            sec("vector", f_ve)

            def f_sc(sc, i=i):
                wt(sc, "sc", "dve", tok[("lnstat", i)])
                a0 = inc(sc.activation(out=sbSTAT[:, 8:9],
                                       in_=sbSTAT[:, 6:7], func=AF.Ln),
                         "act")
                wt(sc, "sc", "act", a0)
                inc(sc.activation(out=sbSTAT[:, 7:8], in_=sbSTAT[:, 8:9],
                                  func=AF.Exp, scale=-0.5), "act")
                tok[("rstd", i)] = cnt["act"]
            sec("scalar", f_sc)

            def f_ve2(ve, i=i):
                wt(ve, "ve", "act", tok[("rstd", i)])
                wt(ve, "ve", "pe", tok.get(("tp", i - 1, 15), 0))
                inc(ve.tensor_scalar(out=A_xn[:], in0=A_h[:],
                                     scalar1=sbSTAT[:, 3:4],
                                     scalar2=sbSTAT[:, 7:8],
                                     op0=ALU.add, op1=ALU.mult), "dve")
                tok[("xn", i)] = cnt["dve"]
            sec("vector", f_ve2)

            for dk in range(16):
                b = 4 + dk % 2

                def f_te(te, i=i, dk=dk, b=b):
                    wt(te, "te", "sy", tok["params"])
                    wt(te, "te", "dve", tok[("xn", i)])
                    bank_wait(te, "te", b)
                    m = te.transpose(P[b][:, 0:128],
                                     in_=A_xn[:, dk * 128:(dk + 1) * 128],
                                     identity=sbID[:])
                    inc(m, "pe")
                    tok[("tp", i, dk)] = cnt["pe"]
                sec("tensor", f_te)

                def f_sc2(sc, i=i, dk=dk, b=b, lt=lt):
                    wt(sc, "sc", "pe", tok[("tp", i, dk)])
                    if i == 0:
                        o, src = xnt(dk)[:, 0:3], P[b][:, 0:3]
                    else:
                        o = xnt(dk)[:, 3 + lt * 128:3 + (lt + 1) * 128]
                        src = P[b][:, 0:128]
                    v = inc(sc.activation(out=o, in_=src, func=AF.Copy), "act")
                    bank[b] = ("act", v)
                sec("scalar", f_sc2)
        XNT_done = cnt["act"]

        def f_wx(sy):
            wt(sy, "sy", "gd", tok[("staged", "wx")])
            inc(sy.dma_start(out=A_wxT[:, 0:5120], in_=wxTb.ap()), "sy", 16)
            tok["wxT"] = cnt["sy"]
            wt(sy, "sy", "gd", tok[("staged", "wdt")])
            inc(sy.dma_start(out=A_wdtT[:, 0:4096], in_=wdtTb.ap()),
                "sy", 16)
            tok["wdtT"] = cnt["sy"]
        sec("sync", f_wx)



        wt_lastmm = [0, 0]

        def win_block_ap(src_all, e0):
            return bass.AP(src_all, e0,
                           [[HF, 128], [128 * HF, 16], [1, 256]])

        for ms in range(16):
            par = ms % 2
            mg = [2 * ms, 2 * ms + 1]

            def f_sy(sy, ms=ms, par=par):
                wt(sy, "sy", "gd", tok[("staged", "winx")])
                wt(sy, "sy", "pe", wt_lastmm[par])
                inc(sy.dma_start(out=WTB[par][:, 0:4096],
                                 in_=win_block_ap(winTb, ms * 256)), "sy", 16)
                tok[("wtx", ms)] = cnt["sy"]
            sec("sync", f_sy)

            def f_te(te, ms=ms, par=par):
                wt(te, "te", "sy", tok[("wtx", ms)])
                wt(te, "te", "act", XNT_done)
                for b in range(4):
                    bank_wait(te, "te", b)
                bank_wait(te, "te", 4)
                m = None
                for k in range(16):
                    for j in range(2):
                        lhs = WTB[par][:, k * 256 + j * 128:
                                       k * 256 + (j + 1) * 128]
                        te.matmul(P[2 * j][:], lhsT=lhs,
                                  rhs=xnt(k)[:, 3:515],
                                  start=(k == 0), stop=(k == 15))
                        m = te.matmul(P[2 * j + 1][:], lhsT=lhs,
                                      rhs=xnt(k)[:, 515:1027],
                                      start=(k == 0), stop=(k == 15))
                inc(m, "pe")
                tok[("xmm", ms)] = cnt["pe"]
                for k in range(16):
                    m = te.matmul(P[4][0:3, 0:256], lhsT=xnt(k)[:, 0:3],
                                  rhs=WTB[par][:, k * 256:(k + 1) * 256],
                                  start=(k == 0), stop=(k == 15))
                inc(m, "pe")
                tok[("hmm", ms)] = cnt["pe"]
                wt_lastmm[par] = cnt["pe"]
            sec("tensor", f_te)

            def f_sc(sc, ms=ms):
                wt(sc, "sc", "pe", tok[("xmm", ms)])
                for j in range(2):
                    wt(sc, "sc", "dve", tok.get(("conv", ms - 1, j), 0))
                    v = inc(sc.activation(out=A_xt[j][:, 3:515],
                                          in_=P[2 * j][:], func=AF.Copy),
                            "act")
                    bank[2 * j] = ("act", v)
                    v = inc(sc.activation(out=A_xt[j][:, 515:1027],
                                          in_=P[2 * j + 1][:], func=AF.Copy),
                            "act")
                    bank[2 * j + 1] = ("act", v)
                wt(sc, "sc", "pe", tok[("hmm", ms)])
                v = inc(sc.activation(out=A_hst[0:3, 0:256],
                                      in_=P[4][0:3, 0:256], func=AF.Copy),
                        "act")
                bank[4] = ("act", v)
                tok[("xev", ms)] = cnt["act"]
            sec("scalar", f_sc)

            def f_ve(ve, ms=ms, mg=mg):
                wt(ve, "ve", "act", tok[("xev", ms)])
                for blk in range(8):
                    ve.transpose(A_scr[0:32, blk * 32:(blk + 1) * 32],
                                 A_hst[0:32, blk * 32:(blk + 1) * 32])
                for blk in range(8):
                    j = blk // 4
                    dbase = (blk % 4) * 32
                    ve.tensor_copy(A_xt[j][dbase:dbase + 32, 0:3],
                                   A_scr[0:32, blk * 32 :blk * 32 + 3])
                for j in range(2):
                    m = mg[j]
                    ve.tensor_scalar_mul(A_cacc[j][:], A_xt[j][:, 0:1024],
                                         sbCW[:, 4 * m:4 * m + 1])
                    last = None
                    for kk in range(1, 4):
                        last = ve.scalar_tensor_tensor(
                            out=A_cacc[j][:], in0=A_xt[j][:, kk:kk + 1024],
                            scalar=sbCW[:, 4 * m + kk:4 * m + kk + 1],
                            in1=A_cacc[j][:], op0=ALU.mult, op1=ALU.add)
                    inc(last, "dve")
                    tok[("conv", ms, j)] = cnt["dve"]
            sec("vector", f_ve)

            def f_sc2(sc, ms=ms, mg=mg):
                for j in range(2):
                    m = mg[j]
                    wt(sc, "sc", "dve", tok[("conv", ms, j)])
                    wt(sc, "sc", "sy", tok.get(("ustore", ms - 1, j), 0))
                    wt(sc, "sc", "dve", tok.get(("xcb", ms - 1, j), 0))
                    inc(sc.activation(out=A_xcf[j][:], in_=A_cacc[j][:],
                                      func=AF.Silu, bias=sbCB[:, m:m + 1]),
                        "act")
                    tok[("xcf", ms, j)] = cnt["act"]
            sec("scalar", f_sc2)

            def f_sy2(sy, ms=ms, mg=mg):
                for j in range(2):
                    m = mg[j]
                    wt(sy, "sy", "act", tok[("xcf", ms, j)])
                    r0 = (m // 4) * 1024 + 512 + (m % 4) * 128
                    inc(sy.dma_start(
                        out=du_in.ap()[r0:r0 + 128, :],
                        in_=A_xcf[j][:]), "sy", 16)
                    tok[("ustore", ms, j)] = cnt["sy"]
            sec("sync", f_sy2)

            def f_ve2(ve, ms=ms, mg=mg):
                for j in range(2):
                    m = mg[j]
                    wt(ve, "ve", "act", tok[("xcf", ms, j)])
                    inc(ve.tensor_copy(A_xcb[:, m * 1024:(m + 1) * 1024],
                                       A_xcf[j][:]), "dve")
                    tok[("xcb", ms, j)] = cnt["dve"]
            sec("vector", f_ve2)
        xcb_done = cnt["dve"]

        def f_wx_te(te):
            wt(te, "te", "sy", tok["wxT"])
            wt(te, "te", "dve", xcb_done)
            for b in range(4):
                bank_wait(te, "te", b)
            m = None
            for k in range(32):
                lhs0 = A_wxT[:, k * 160:k * 160 + 128]
                lhs1 = A_wxT[:, k * 160 + 128:(k + 1) * 160]
                mv0 = A_xcb[:, k * 1024:k * 1024 + 512]
                mv1 = A_xcb[:, k * 1024 + 512:(k + 1) * 1024]
                te.matmul(P[0][:], lhsT=lhs0, rhs=mv0, start=(k == 0),
                          stop=(k == 31))
                te.matmul(P[1][:], lhsT=lhs0, rhs=mv1, start=(k == 0),
                          stop=(k == 31))
                te.matmul(P[2][0:32, :], lhsT=lhs1, rhs=mv0, start=(k == 0),
                          stop=(k == 31))
                m = te.matmul(P[3][0:32, :], lhsT=lhs1, rhs=mv1,
                              start=(k == 0), stop=(k == 31))
            inc(m, "pe")
            tok["wx_mm"] = cnt["pe"]
        sec("tensor", f_wx_te)

        def f_wx_sc(sc):
            wt(sc, "sc", "pe", tok["wx_mm"])
            v = inc(sc.activation(out=A_dtp[:, 0:512], in_=P[0][:],
                                  func=AF.Copy), "act")
            bank[0] = ("act", v)
            v = inc(sc.activation(out=A_dtp[:, 512:1024], in_=P[1][:],
                                  func=AF.Copy), "act")
            bank[1] = ("act", v)
            v = inc(sc.activation(out=A_bc[0:32, 0:512], in_=P[2][0:32, :],
                                  func=AF.Copy), "act")
            bank[2] = ("act", v)
            v = inc(sc.activation(out=A_bc[0:32, 512:1024],
                                  in_=P[3][0:32, :], func=AF.Copy), "act")
            bank[3] = ("act", v)
            tok["bc_ev"] = cnt["act"]
        sec("scalar", f_wx_sc)

        def f_b_store(sy):
            wt(sy, "sy", "act", tok["bc_ev"])
            o = bass.AP(bc_in, 0, [[1024, 16], [1, 1024]])
            inc(sy.dma_start(out=o, in_=A_bc[0:16, :]), "sy", 16)
        sec("sync", f_b_store)

        for c8 in range(8):
            def f_t(te, c8=c8):
                wt(te, "te", "act", tok["bc_ev"])
                bank_wait(te, "te", 4)
                m = te.transpose(P[4][:, 0:32],
                                 in_=A_bc[0:32, c8 * 128:(c8 + 1) * 128],
                                 identity=sbID[0:32, 0:32])
                inc(m, "pe")
                tok[("ctp", c8)] = cnt["pe"]
            sec("tensor", f_t)

            def f_s(sc, c8=c8):
                wt(sc, "sc", "pe", tok[("ctp", c8)])
                v = inc(sc.activation(out=A_ct[:, c8 * 16:(c8 + 1) * 16],
                                      in_=P[4][:, 16:32], func=AF.Copy),
                        "act")
                bank[4] = ("act", v)
            sec("scalar", f_s)
        tok["ct_ev"] = cnt["act"]

        def f_c_store(sy):
            wt(sy, "sy", "act", tok["ct_ev"])
            o = bass.AP(bc_in, 16384, [[16, 128], [2048, 8], [1, 16]])
            inc(sy.dma_start(out=o, in_=A_ct[:]), "sy", 16)
        sec("sync", f_c_store)

        last_dl = [0]
        for m in range(32):
            mr = m % 2

            def f_t(te, m=m):
                wt(te, "te", "sy", tok["wdtT"])
                wt(te, "te", "act", tok["bc_ev"])
                bank_wait(te, "te", 5)
                bank_wait(te, "te", 6)
                lhs = A_wdtT[:, m * 128:(m + 1) * 128]
                te.matmul(P[5][:], lhsT=lhs, rhs=A_dtp[:, 0:512],
                          start=True, stop=True)
                m2 = te.matmul(P[6][:], lhsT=lhs, rhs=A_dtp[:, 512:1024],
                               start=True, stop=True)
                inc(m2, "pe")
                tok[("dtmm", m)] = cnt["pe"]
            sec("tensor", f_t)

            def f_s(sc, m=m):
                wt(sc, "sc", "pe", tok[("dtmm", m)])
                wt(sc, "sc", "dve", last_dl[0])
                bcol = sbBDT[:, m:m + 1]
                SP0 = [A_sp[0][:], A_cacc[0][:, 0:512]]
                SP1 = [A_sp[1][:], A_cacc[0][:, 512:1024]]
                SP2 = [A_sp[2][:], A_cacc[1][:, 0:512]]
                for ch in range(2):
                    sc.activation(out=SP0[ch], in_=P[5 + ch][:],
                                  func=AF.Relu, bias=bcol)
                for ch in range(2):
                    sc.activation(out=SP1[ch], in_=P[5 + ch][:],
                                  func=AF.Abs, bias=bcol)
                for ch in range(2):
                    sc.activation(out=SP2[ch], in_=SP1[ch],
                                  func=AF.Exp, scale=-1.0)
                v = None
                for ch in range(2):
                    v = inc(sc.activation(out=SP1[ch], in_=SP2[ch],
                                          func=AF.Ln, bias=1.0), "act")
                bank[5] = ("act", v)
                bank[6] = ("act", v)
                tok[("spl", m)] = cnt["act"]
            sec("scalar", f_s)

            def f_v(ve, m=m, mr=mr):
                SP0 = [A_sp[0][:], A_cacc[0][:, 0:512]]
                SP1 = [A_sp[1][:], A_cacc[0][:, 512:1024]]
                wt(ve, "ve", "act", tok[("spl", m)])
                wt(ve, "ve", "sy", tok.get(("dstore", m - 2), 0))
                for ch in range(2):
                    inc(ve.tensor_add(A_dl[mr][:, ch * 512:(ch + 1) * 512],
                                      SP1[ch], SP0[ch]), "dve")
                tok[("dl", m, 1)] = cnt["dve"]
                last_dl[0] = cnt["dve"]
            sec("vector", f_v)

            def f_y(sy, m=m, mr=mr):
                wt(sy, "sy", "dve", tok[("dl", m, 1)])
                r0 = (m // 4) * 1024 + (m % 4) * 128
                inc(sy.dma_start(out=du_in.ap()[r0:r0 + 128, :],
                                 in_=A_dl[mr][:]), "sy", 16)
                tok[("dstore", m)] = cnt["sy"]
            sec("sync", f_y)
        tok["du_done"] = cnt["sy"]

        def f_agdu(po):
            wt(po, "po", "sy", tok["du_done"])
            wt(po, "po", "gd", tok[("staged", "wu", 0)])
            m = po.collective_compute("AllToAll", ALU.bypass,
                                      replica_groups=RG,
                                      ins=[du_in.ap()], outs=[du_out.ap()])
            inc(m, "cc")
            m = po.collective_compute("AllGather", ALU.bypass,
                                      replica_groups=RG,
                                      ins=[bc_in.ap()], outs=[bc_all.ap()])
            inc(m, "cc")
            wt(po, "po", "cc", cnt["cc"])
            m = po.collective_compute("AllGather", ALU.bypass,
                                      replica_groups=RG,
                                      ins=[fence_in.ap()],
                                      outs=[fence_out[0].ap()])
            inc(m, "cc")
            tok["cc_du"] = cnt["cc"]
            tok["cc_bc"] = cnt["cc"]
        sec("gpsimd", f_agdu)

        for ms in range(16):
            par = ms % 2

            def f_sy(sy, ms=ms, par=par):
                wt(sy, "sy", "gd", tok[("staged", "winz")])
                wt(sy, "sy", "pe", wt_lastmm[par])
                inc(sy.dma_start(out=WTB[par][:, 0:4096],
                                 in_=win_block_ap(winTb, 4096 + ms * 256)),
                    "sy", 16)
                tok[("wtz", ms)] = cnt["sy"]
            sec("sync", f_sy)

            def f_te(te, ms=ms, par=par):
                wt(te, "te", "sy", tok[("wtz", ms)])
                for b in range(4):
                    bank_wait(te, "te", b)
                m = None
                for k in range(16):
                    for j in range(2):
                        lhs = WTB[par][:, k * 256 + j * 128:
                                       k * 256 + (j + 1) * 128]
                        te.matmul(P[2 * j][:], lhsT=lhs,
                                  rhs=xnt(k)[:, 3:515],
                                  start=(k == 0), stop=(k == 15))
                        m = te.matmul(P[2 * j + 1][:], lhsT=lhs,
                                      rhs=xnt(k)[:, 515:1027],
                                      start=(k == 0), stop=(k == 15))
                inc(m, "pe")
                tok[("zmm", ms)] = cnt["pe"]
                wt_lastmm[par] = cnt["pe"]
            sec("tensor", f_te)

            for j in range(2):
                def f_sc(sc, ms=ms, j=j):
                    wt(sc, "sc", "pe", tok[("zmm", ms)])
                    wt(sc, "sc", "sy", tok.get(("zstore", ms, j - 1),
                                               tok.get(("zstore", ms - 1, 1),
                                                       0)))
                    v = inc(sc.activation(out=F_z[:, 0:512],
                                          in_=P[2 * j][:], func=AF.Copy),
                            "act")
                    bank[2 * j] = ("act", v)
                    v = inc(sc.activation(out=F_z[:, 512:1024],
                                          in_=P[2 * j + 1][:], func=AF.Copy),
                            "act")
                    bank[2 * j + 1] = ("act", v)
                    tok[("zev", ms, j)] = cnt["act"]
                sec("scalar", f_sc)

                def f_sy2(sy, ms=ms, j=j):
                    wt(sy, "sy", "act", tok[("zev", ms, j)])
                    mz = 2 * ms + j
                    inc(sy.dma_start(
                        out=z_buf.ap()[mz * 128:(mz + 1) * 128, :],
                        in_=F_z[:]), "sy", 16)
                    tok[("zstore", ms, j)] = cnt["sy"]
                sec("sync", f_sy2)

        zlast = (wt_lastmm[0], wt_lastmm[1])

        stok = {}
        gu_tok = {}

        def gu_ms(ms):
            gt = [WTB[0], WTB[1]] if ms % 2 == 0 else [WTBg2[0], WTBg2[1]]

            def f_sy(sy, ms=ms, gt=gt):
                wt(sy, "sy", "gd", tok[("staged", "wg", ms // 8)])
                wt(sy, "sy", "pe", gu_tok.get(("gmm", ms - 2), zlast[0]))
                inc(sy.dma_start(out=gt[0][:, 0:4096],
                                 in_=win_block_ap(wgTb, ms * 256)), "sy", 16)
                gu_tok[("wg", ms)] = cnt["sy"]
                wt(sy, "sy", "gd", tok[("staged", "wu", ms // 8)])
                wt(sy, "sy", "pe", gu_tok.get(("umm", ms - 2), zlast[1]))
                inc(sy.dma_start(out=gt[1][:, 0:4096],
                                 in_=win_block_ap(wuTb, ms * 256)), "sy", 16)
                gu_tok[("wu", ms)] = cnt["sy"]
            sec("sync", f_sy)

            def f_te(te, ms=ms, gt=gt):
                wt(te, "te", "sy", gu_tok[("wg", ms)])
                for b in range(8):
                    bank_wait(te, "te", b)
                m = None
                for k in range(16):
                    for j in range(2):
                        lhs = gt[0][:, k * 256 + j * 128:
                                    k * 256 + (j + 1) * 128]
                        te.matmul(P[2 * j][:], lhsT=lhs,
                                  rhs=xnt(k)[:, 3:515],
                                  start=(k == 0), stop=(k == 15))
                        m = te.matmul(P[2 * j + 1][:], lhsT=lhs,
                                      rhs=xnt(k)[:, 515:1027],
                                      start=(k == 0), stop=(k == 15))
                inc(m, "pe")
                gu_tok[("gmm", ms)] = cnt["pe"]
                wt(te, "te", "sy", gu_tok[("wu", ms)])
                for k in range(16):
                    for j in range(2):
                        lhs = gt[1][:, k * 256 + j * 128:
                                    k * 256 + (j + 1) * 128]
                        te.matmul(P[4 + 2 * j][:], lhsT=lhs,
                                  rhs=xnt(k)[:, 3:515],
                                  start=(k == 0), stop=(k == 15))
                        m = te.matmul(P[5 + 2 * j][:], lhsT=lhs,
                                      rhs=xnt(k)[:, 515:1027],
                                      start=(k == 0), stop=(k == 15))
                inc(m, "pe")
                gu_tok[("umm", ms)] = cnt["pe"]
            sec("tensor", f_te)

            def f_sc(sc, ms=ms):
                w2 = ms % 2
                wt(sc, "sc", "pe", gu_tok[("gmm", ms)])
                wt(sc, "sc", "sy", gu_tok.get(("gst_store", ms - 2), 0))
                for q in range(4):
                    v = inc(sc.activation(out=Fg[w2][q][:], in_=P[q][:],
                                          func=AF.Silu), "act")
                    bank[q] = ("act", v)
                wt(sc, "sc", "pe", gu_tok[("umm", ms)])
                for q in range(4):
                    v = inc(sc.activation(out=Fu[w2][q][:], in_=P[4 + q][:],
                                          func=AF.Copy), "act")
                    bank[4 + q] = ("act", v)
                gu_tok[("guev", ms)] = cnt["act"]
            sec("scalar", f_sc)

            def f_sy2(sy, ms=ms):
                w2 = ms % 2
                wt(sy, "sy", "act", gu_tok[("guev", ms)])
                for j in range(2):
                    r0 = (2 * ms + j) * 128
                    for jj in range(2):
                        q = 2 * j + jj
                        inc(sy.dma_start(
                            out=g_buf.ap()[r0:r0 + 128,
                                           jj * 512:(jj + 1) * 512],
                            in_=Fg[w2][q][:]), "sy", 16)
                        inc(sy.dma_start(
                            out=u_buf.ap()[r0:r0 + 128,
                                           jj * 512:(jj + 1) * 512],
                            in_=Fu[w2][q][:]), "sy", 16)
                gu_tok[("gst_store", ms)] = cnt["sy"]
            sec("sync", f_sy2)

        def scan_w(w):
            par = w % 2

            def f_ld(sy, w=w, par=par):
                rb_i = w // 4
                l0 = w * W - rb_i * 1024
                wt(sy, "sy", "cc", tok["cc_du"])
                wt(sy, "sy", "act", stok.get(("aexp_done", w - 2), 0))
                duo = rb_i * (1024 * T) + l0
                inc(sy.dma_start(
                    out=S_dsl[par][:],
                    in_=bass.AP(du_out, duo,
                                [[T, 128], [128 * T, 4], [1, W]])),
                    "sy", 16)
                stok[("dsl", w)] = cnt["sy"]
                wt(sy, "sy", "dve", stok.get(("stt_done", w - 1), 0))
                inc(sy.dma_start(
                    out=S_usl[:],
                    in_=bass.AP(du_out, duo + 512 * T,
                                [[T, 128], [128 * T, 4], [1, W]])),
                    "sy", 16)
                stok[("usl", w)] = cnt["sy"]
                wt(sy, "sy", "cc", tok["cc_bc"])
                wt(sy, "sy", "dve", stok.get(("b_done", w - 1), 0))
                offb = rb_i * 256 * 128
                inc(sy.dma_start(
                    out=S_Br[:],
                    in_=bass.AP(bc_all, offb + l0,
                                [[0, 128], [1024, 16], [1, W]])), "sy", 16)
                stok[("Br", w)] = cnt["sy"]
                wt(sy, "sy", "dve", stok.get(("cm_done", w - 1), 0))
                inc(sy.dma_start(
                    out=S_Cr[:],
                    in_=bass.AP(bc_all, offb + 16384 + l0 * 16,
                                [[0, 128], [16, W], [1, 16]])), "sy", 16)
                stok[("Cr", w)] = cnt["sy"]
            sec("sync", f_ld)

            def f_du(ve, w=w, par=par):
                wt(ve, "ve", "sy", stok[("usl", w)])
                inc(ve.tensor_mul(S_du[:], S_dsl[par][:], S_usl[:]), "dve")
                stok[("du", w)] = cnt["dve"]
            sec("vector", f_du)

            for t in range(NT):
                idx = w * NT + t
                spar = idx % 2

                def f_a(sc, w=w, t=t, par=par):
                    wt(sc, "sc", "sy", stok[("dsl", w)])
                    wt(sc, "sc", "dve", tok["a_neg"])
                    wt(sc, "sc", "dve", stok.get(("scans", w * NT + t - 2), 0))
                    base = cnt["act"]
                    for n in range(NST):
                        acol = sbAn[:, t * NST + n:t * NST + n + 1]
                        inc(sc.activation(
                            out=S_a[t % 2][:, n * W:(n + 1) * W],
                            in_=S_dsl[par][:, t * W:(t + 1) * W],
                            func=AF.Exp, scale=acol), "act")
                    stok[("aexp", w, t)] = base
                    if t == NT - 1:
                        stok[("aexp_done", w)] = cnt["act"]
                sec("scalar", f_a)

                def f_b(ve, w=w, t=t):
                    wt(ve, "ve", "sy", stok[("Br", w)])
                    du3 = S_du[:, t * W:(t + 1) * W].rearrange(
                        "p (o l) -> p o l", o=1).broadcast_to((128, NST, W))
                    inc(ve.tensor_tensor(
                        out=S_b[:].rearrange("p (n l) -> p n l", n=NST),
                        in0=S_Br[:].rearrange("p (n l) -> p n l", n=NST),
                        in1=du3,
                        op=ALU.mult), "dve")
                    stok[("b", w, t)] = cnt["dve"]
                    if t == NT - 1:
                        stok[("b_done", w)] = cnt["dve"]
                sec("vector", f_b)

                def f_scan(ve, w=w, t=t, idx=idx, spar=spar):
                    ab = stok[("aexp", w, t)]
                    wt(ve, "ve", "act", ab + NST)
                    sview = S_s[spar][:].rearrange("p (l n) -> p n l", n=NST)
                    last = None
                    for n in range(NST):
                        init = (0.0 if w == 0 else
                                S_carry[:, t * NST + n:t * NST + n + 1])
                        last = ve.tensor_tensor_scan(
                            out=sview[:, n, :],
                            data0=S_a[t % 2][:, n * W:(n + 1) * W],
                            data1=S_b[:, n * W:(n + 1) * W],
                            initial=init, op0=ALU.mult, op1=ALU.add)
                    v_sc = inc(last, "dve")
                    stok[("scans", idx)] = v_sc
                    wt(ve, "ve", "dve", v_sc)
                    inc(ve.tensor_copy(S_carry[:, t * NST:(t + 1) * NST],
                                       S_s[spar][:, (W - 1) * NST:W * NST]),
                        "dve")
                sec("vector", f_scan)

                def f_cm(ve, w=w, t=t, idx=idx, spar=spar):
                    wt(ve, "ve", "sy", stok[("Cr", w)])
                    inc(ve.tensor_mul(S_s[spar][:], S_s[spar][:],
                                      S_Cr[:]), "dve")
                    if t == NT - 1:
                        stok[("cm_done", w)] = cnt["dve"]
                sec("vector", f_cm)

                def f_red(ve, w=w, t=t, idx=idx, spar=spar):
                    red_src = S_s[spar]
                    wt(ve, "ve", "sy", stok.get(("ystore", w - 1), 0))
                    ve.tensor_reduce(
                        out=S_y[:, t * W:(t + 1) * W],
                        in_=red_src[:].rearrange("p (l n) -> p l n", n=NST),
                        axis=AX.X, op=ALU.add)
                    inc(ve.scalar_tensor_tensor(
                        out=S_y[:, t * W:(t + 1) * W],
                        in0=S_usl[:, t * W:(t + 1) * W],
                        scalar=sbD[:, t:t + 1],
                        in1=S_y[:, t * W:(t + 1) * W],
                        op0=ALU.mult, op1=ALU.add), "dve")
                    stok[("red", idx)] = cnt["dve"]
                    if t == NT - 1:
                        stok[("stt_done", w)] = cnt["dve"]
                sec("vector", f_red)

            def f_ys(sy, w=w):
                wt(sy, "sy", "dve", stok[("stt_done", w)])
                o = bass.AP(y_in, (w // 4) * (S * T) + (w % 4) * W,
                            [[T, 128], [128 * T, 4], [1, W]])
                inc(sy.dma_start(out=o, in_=S_y[:]), "sy", 16)
                stok[("ystore", w)] = cnt["sy"]
            sec("sync", f_ys)


        dn = {}

        def down_chunk_head(ci):
            pr, kh = divmod(ci, 4)
            lts = [2 * pr, 2 * pr + 1]

            def f_gld(sy, ci=ci, pr=pr, kh=kh, lts=lts):
                wt(sy, "sy", "sy", gu_tok[("gst_store", kh * 8 + 7)])
                wt(sy, "sy", "pe", gu_tok[("umm", 31)])
                for i, lt in enumerate(lts):
                    wt(sy, "sy", "dve", dn.get(("prod", ci - 1, i), 0))
                    ag = bass.AP(g_buf, kh * 16 * 128 * T + lt * 128,
                                 [[T, 128], [128 * T, 16], [1, 128]])
                    au = bass.AP(u_buf, kh * 16 * 128 * T + lt * 128,
                                 [[T, 128], [128 * T, 16], [1, 128]])
                    inc(sy.dma_start(out=Fg_ld[i][:], in_=ag), "sy", 16)
                    inc(sy.dma_start(out=Fu_ld[i][:], in_=au), "sy", 16)
                    dn[("gld", ci, i)] = cnt["sy"]
            sec("sync", f_gld)

            def f_prod(ve, ci=ci, pr=pr, kh=kh):
                wt(ve, "ve", "sy", gu_tok[("gst_store", 31)])
                prev = dn.get(("mmlast",) + divmod(ci - 1, 4), 0)
                wt(ve, "ve", "pe", prev)
                for i in range(2):
                    wt(ve, "ve", "sy", dn[("gld", ci, i)])
                    inc(ve.tensor_mul(F_glt[i][:], Fg_ld[i][:],
                                      Fu_ld[i][:]), "dve")
                    dn[("prod", ci, i)] = cnt["dve"]
            sec("vector", f_prod)

        def down_chunk_tail(ci):
            pr, kh = divmod(ci, 4)
            for k in range(16):
                kk = kh * 16 + k
                spar = kk % 2

                def f_sl(sc, ci=ci, pr=pr, kh=kh, k=k, kk=kk, spar=spar):
                    wt(sc, "sc", "gd", tok["repack_wdn"])
                    if ci == 0 and k < 2:
                        wt(sc, "sc", "sy", gu_tok[("gst_store", 31)])
                    wt(sc, "sc", "pe", dn.get(("kmm", pr, kh, k - 2), 0))
                    if k < 2:
                        wt(sc, "sc", "pe",
                           dn.get(("kmm", pr, kh - 1, 14 + k),
                                  dn.get(("kmm", pr - 1, 3, 14 + k), 0)))
                    inc(sc.dma_start(
                        out=F_dstr[spar][:],
                        in_=wdnTc.ap()[kk * 128:(kk + 1) * 128, :]),
                        "ad", 16)
                    dn[("str", pr, kh, k)] = cnt["ad"]
                sec("scalar", f_sl)

                def f_km(te, ci=ci, pr=pr, kh=kh, k=k, kk=kk, spar=spar):
                    for i in range(2):
                        wt(te, "te", "dve", dn[("prod", ci, i)])
                    wt(te, "te", "ad", dn[("str", pr, kh, k)])
                    if kk == 0:
                        for b in range(8):
                            bank_wait(te, "te", b)
                    m = None
                    for i in range(2):
                        lhs = F_glt[i][:, k * 128:(k + 1) * 128]
                        for c in range(4):
                            m = te.matmul(
                                P[4 * i + c][:], lhsT=lhs,
                                rhs=F_dstr[spar][:, c * 512:(c + 1) * 512],
                                start=(kk == 0), stop=(kk == 63))
                    inc(m, "pe")
                    dn[("kmm", pr, kh, k)] = cnt["pe"]
                sec("tensor", f_km)
            dn[("mmlast", pr, kh)] = cnt["pe"]

        def down_evict(pr):
            lts = [2 * pr, 2 * pr + 1]
            for i, lt in enumerate(lts):
                for c in range(4):
                    def f_ev(sc, pr=pr, i=i, c=c):
                        wt(sc, "sc", "pe", dn[("mmlast", pr, 3)])
                        jp = 4 * i + c - 2
                        wk = (("fstore", pr, jp // 4, jp % 4) if jp >= 0
                              else ("fstore", pr - 1, (jp + 8) // 4,
                                    (jp + 8) % 4))
                        wt(sc, "sc", "sy", dn.get(wk, 0))
                        v = inc(sc.activation(out=F_ffn[c % 2][:],
                                              in_=P[4 * i + c][:],
                                              func=AF.Copy), "act")
                        bank[4 * i + c] = ("act", v)
                        dn[("fev", pr, i, c)] = cnt["act"]
                    sec("scalar", f_ev)

                    def f_fs(sy, pr=pr, i=i, c=c, lt=lt):
                        wt(sy, "sy", "act", dn[("fev", pr, i, c)])
                        inc(sy.dma_start(
                            out=ffn_buf.ap()[lt * 128:(lt + 1) * 128,
                                             c * 512:(c + 1) * 512],
                            in_=F_ffn[c % 2][:]), "sy", 16)
                        dn[("fstore", pr, i, c)] = cnt["sy"]
                    sec("sync", f_fs)
            if pr == 3:
                tok["ffn_done"] = cnt["sy"]

        DS = 16
        for w in range(NW):
            ci = w - DS
            if w == 16:
                gu_ms(30)
                gu_ms(31)
            if 0 <= ci < 16:
                down_chunk_head(ci)
            scan_w(w)
            if 1 <= w <= 15:
                gu_ms(2 * (w - 1))
                gu_ms(2 * w - 1)
            if 0 <= w <= 10:
                def f_cast_late(po, w=w):
                    wt(po, "po", "cc", tok["cc_du"])
                    if w >= 1:
                        wt(po, "po", "dve", stok[("stt_done", w - 1)])
                    if w < 3:
                        ci2 = w + 1
                        rs = slice(ci2 * 2048, (ci2 + 1) * 2048)
                        inc(po.dma_start(out=wgTb.ap()[:, rs],
                                         in_=wgT[:, rs]), "gd", 16)
                        tok[("staged", "wg", ci2)] = cnt["gd"]
                        wt(po, "po", "gd", cnt["gd"])
                        inc(po.dma_start(out=wuTb.ap()[:, rs],
                                         in_=wuT[:, rs]), "gd", 16)
                        tok[("staged", "wu", ci2)] = cnt["gd"]
                        wt(po, "po", "gd", cnt["gd"])
                    elif w < 7:
                        pc = w - 3
                        rs = slice(pc * 2048, (pc + 1) * 2048)
                        inc(po.dma_start(out=wdnTc.ap()[rs, :],
                                         in_=wdnT[rs, :]), "gd", 16)
                        wt(po, "po", "gd", cnt["gd"])
                        if pc == 3:
                            tok["repack_wdn"] = cnt["gd"]
                    else:
                        pc = w - 7
                        rs = slice(pc * 1024, (pc + 1) * 1024)
                        inc(po.dma_start(out=woutTc.ap()[rs, :],
                                         in_=woutT[rs, :]), "gd", 16)
                        wt(po, "po", "gd", cnt["gd"])
                        if pc == 10 - 7:
                            tok["repack_wout"] = cnt["gd"]
                sec("gpsimd", f_cast_late)
            if 0 <= ci < 16:
                down_chunk_tail(ci)
            if w >= DS + 3 and (w - DS - 3) % 4 == 0 and (w - DS - 3) // 4 < 4:
                down_evict((w - DS - 3) // 4)

        def f_warm(te):
            for b in range(8):
                bank_wait(te, "te", b)
            m = None
            for _ in range(800):
                m = te.matmul(P[7][:], lhsT=P5_wo[0][:, 0:128],
                              rhs=P5_wo[1][:, 0:512], start=True, stop=True)
            inc(m, "pe")
            tok["warm"] = cnt["pe"]
        sec("tensor", f_warm)

        def f_agy(po):
            wt(po, "po", "sy", stok[("ystore", NW - 1)])
            wt(po, "po", "sy", tok["ffn_done"])
            wt(po, "po", "pe", dn[("mmlast", 3, 3)])
            m = po.collective_compute("AllToAll", ALU.bypass,
                                      replica_groups=RG,
                                      ins=[y_in.ap()], outs=[y_loc.ap()])
            inc(m, "cc")
            wt(po, "po", "cc", cnt["cc"])
            m = po.collective_compute("AllGather", ALU.bypass,
                                      replica_groups=RG,
                                      ins=[fence_in.ap()],
                                      outs=[fence_out[1].ap()])
            inc(m, "cc")
            tok["cc_y"] = cnt["cc"]
        sec("gpsimd", f_agy)

        p5 = {}
        for k in range(32):
            zr = k % 2

            def f_sy(sy, k=k, zr=zr):
                wt(sy, "sy", "cc", tok["cc_y"])
                wt(sy, "sy", "dve", p5.get(("y2", k - 2), 0))
                inc(sy.dma_start(out=P5_zs[zr][:],
                                 in_=z_buf.ap()[k * 128:(k + 1) * 128, :]),
                    "sy", 16)
                p5[("z", k)] = cnt["sy"]
                inc(sy.dma_start(out=P5_y[zr][:],
                                 in_=y_loc.ap()[k * 128:(k + 1) * 128, :]),
                    "sy", 16)
                p5[("y", k)] = cnt["sy"]
            sec("sync", f_sy)

            def f_sc(sc, k=k, zr=zr):
                wt(sc, "sc", "sy", p5[("z", k)])
                wt(sc, "sc", "dve", p5.get(("y2", k - 2), 0))
                inc(sc.activation(out=P5_zs[zr][:], in_=P5_zs[zr][:],
                                  func=AF.Silu), "act")
                p5[("silu", k)] = cnt["act"]
            sec("scalar", f_sc)

            def f_ve(ve, k=k, zr=zr):
                wt(ve, "ve", "act", p5[("silu", k)])
                wt(ve, "ve", "sy", p5[("y", k)])
                inc(ve.tensor_mul(P5_y2[:, k * 1024:(k + 1) * 1024],
                                  P5_y[zr][:], P5_zs[zr][:]), "dve")
                p5[("y2", k)] = cnt["dve"]
            sec("vector", f_ve)
        y2_done = cnt["dve"]

        def f_brow(sy):
            wt(sy, "sy", "cc", tok["cc_y"])
            inc(sy.dma_start(out=P5_hb[0:1, 0:2048], in_=bout), "sy", 16)
            tok["bout_ld"] = cnt["sy"]
        sec("sync", f_brow)

        def f_brep_te(te):
            wt(te, "te", "sy", tok["bout_ld"])
            for b in range(4):
                bank_wait(te, "te", b)
            m = None
            for c in range(4):
                m = te.matmul(P[c][:], lhsT=sbONES[:],
                              rhs=P5_hb[0:1, c * 512:(c + 1) * 512],
                              start=True, stop=True)
            inc(m, "pe")
            tok["brep_mm"] = cnt["pe"]
        sec("tensor", f_brep_te)

        def f_brep_sc(sc):
            wt(sc, "sc", "pe", tok["brep_mm"])
            for c in range(4):
                v = inc(sc.activation(out=P5_brep[:, c * 512:(c + 1) * 512],
                                      in_=P[c][:], func=AF.Copy), "act")
                bank[c] = ("act", v)
            tok["brep"] = cnt["act"]
        sec("scalar", f_brep_sc)

        for pr in range(4):
            lts = [2 * pr, 2 * pr + 1]
            for k in range(32):
                spar = k % 2

                def f_wl(sy, pr=pr, k=k, spar=spar):
                    wt(sy, "sy", "gd", tok["repack_wout"])
                    wt(sy, "sy", "pe", p5.get(("womm", pr, k - 2),
                                              p5.get(("womm", pr - 1, 30 + k),
                                                     0)))
                    inc(sy.dma_start(
                        out=P5_wo[spar][:],
                        in_=woutTc.ap()[k * 128:(k + 1) * 128, :]), "sy", 16)
                    p5[("wo", pr, k)] = cnt["sy"]
                sec("sync", f_wl)

                def f_wm(te, pr=pr, k=k, spar=spar, lts=lts):
                    wt(te, "te", "dve", y2_done)
                    wt(te, "te", "sy", p5[("wo", pr, k)])
                    if k == 0:
                        for b in range(8):
                            bank_wait(te, "te", b)
                    m = None
                    for i in range(2):
                        lhs = P5_y2[:, k * 1024 + lts[i] * 128:
                                    k * 1024 + (lts[i] + 1) * 128]
                        for c in range(4):
                            m = te.matmul(
                                P[4 * i + c][:], lhsT=lhs,
                                rhs=P5_wo[spar][:, c * 512:(c + 1) * 512],
                                start=(k == 0), stop=(k == 31))
                    inc(m, "pe")
                    p5[("womm", pr, k)] = cnt["pe"]
                sec("tensor", f_wm)
            for i in range(2):
                p5[("wmm", pr, i)] = cnt["pe"]

            for i, lt in enumerate(lts):
                def f_ld(sy, pr=pr, i=i, lt=lt):
                    wt(sy, "sy", "sy", tok["ffn_done"])
                    wt(sy, "sy", "pe", tok["brep_mm"])
                    if i == 0:
                        wt(sy, "sy", "dve", p5.get(("comb", pr - 1, 1), 0))
                    else:
                        wt(sy, "sy", "dve", p5[("comb", pr, 0)])
                    inc(sy.dma_start(
                        out=P5_ffn[:],
                        in_=ffn_buf.ap()[lt * 128:(lt + 1) * 128, :]),
                        "sy", 16)
                    p5[("ffl", pr, i)] = cnt["sy"]
                    inc(sy.dma_start(out=P5_hb[:],
                                     in_=h_main[lt * 128:(lt + 1) * 128, :]),
                        "sy", 16)
                    p5[("hl", pr, i)] = cnt["sy"]
                sec("sync", f_ld)

                def f_cb(ve, pr=pr, i=i):
                    orr = i % 2
                    wt(ve, "ve", "pe", p5[("wmm", pr, i)])
                    wt(ve, "ve", "sy", p5[("hl", pr, i)])
                    wt(ve, "ve", "act", tok["brep"])
                    wt(ve, "ve", "sy", p5.get(("ostore", pr - 1, i), 0))
                    ve.tensor_add(P5_hb[:], P5_hb[:], P5_brep[:])
                    for c in range(4):
                        ve.tensor_add(P5_out[orr][:, c * 512:(c + 1) * 512],
                                      P[4 * i + c][:],
                                      P5_ffn[:, c * 512:(c + 1) * 512])
                    v = inc(ve.tensor_add(P5_out[orr][:], P5_out[orr][:],
                                          P5_hb[:]), "dve")
                    for c in range(4):
                        bank[4 * i + c] = ("dve", v)
                    p5[("comb", pr, i)] = cnt["dve"]
                sec("vector", f_cb)

                def f_os(sy, pr=pr, i=i, lt=lt):
                    orr = i % 2
                    wt(sy, "sy", "dve", p5[("comb", pr, i)])
                    inc(sy.dma_start(out=out_o[lt * 128:(lt + 1) * 128, :],
                                     in_=P5_out[orr][:]), "sy", 16)
                    p5[("ostore", pr, i)] = cnt["sy"]
                sec("sync", f_os)

        def f_final(sy):
            sy.wait_ge(sems["sy"], cnt["sy"])
        sec("sync", f_final)

    return nc


_NC_CACHE = {}


def kernel(L0_attn_hidden_states, L0_attn_W_in, L0_attn_conv_w, L0_attn_conv_b,
           L0_attn_W_x, L0_attn_W_dt, L0_attn_b_dt, L0_attn_A_log, L0_attn_D,
           L0_attn_W_out, L0_attn_b_out, L0_ffn_w_gate, L0_ffn_w_up,
           L0_ffn_w_down):
    h = np.ascontiguousarray(np.asarray(L0_attn_hidden_states, np.float32)
                             .reshape(8192, 2048))
    Win = np.ascontiguousarray(np.asarray(L0_attn_W_in, np.float32))
    Wx = np.ascontiguousarray(np.asarray(L0_attn_W_x, np.float32))
    Wdt = np.ascontiguousarray(np.asarray(L0_attn_W_dt, np.float32))
    Wout = np.ascontiguousarray(np.asarray(L0_attn_W_out, np.float32))
    Wg = np.ascontiguousarray(np.asarray(L0_ffn_w_gate, np.float32))
    Wu = np.ascontiguousarray(np.asarray(L0_ffn_w_up, np.float32))
    Wd = np.ascontiguousarray(np.asarray(L0_ffn_w_down, np.float32))
    cw = np.asarray(L0_attn_conv_w, np.float32)
    cb = np.asarray(L0_attn_conv_b, np.float32)
    bdt_ = np.asarray(L0_attn_b_dt, np.float32)
    Alog = np.asarray(L0_attn_A_log, np.float32)
    Dv = np.asarray(L0_attn_D, np.float32)
    bo = np.asarray(L0_attn_b_out, np.float32)

    wx_h = np.ascontiguousarray(
        Wx.T.reshape(32, 128, 160).transpose(1, 0, 2).reshape(128, 5120))
    wdt_h = np.ascontiguousarray(Wdt.T)
    winT_h = np.ascontiguousarray(Win.T)
    wgT_h = np.ascontiguousarray(Wg.T)
    wuT_h = np.ascontiguousarray(Wu.T)
    wdnT_h = np.ascontiguousarray(Wd.T)
    woutT_h = np.ascontiguousarray(Wout.T)
    convw_h = np.ascontiguousarray(
        cw.reshape(32, 128, 4).transpose(1, 0, 2).reshape(128, 128))
    convb_h = np.ascontiguousarray(cb.reshape(32, 128).T)
    bdt_h = np.ascontiguousarray(bdt_.reshape(32, 128).T)
    ones_h = np.ones((1, 128), np.float32)
    id_h = np.eye(128, dtype=np.float32)
    bout_h = np.ascontiguousarray(bo.reshape(1, 2048))

    in_maps = []
    for c in range(NCORES):
        halo = (np.zeros((3, 2048), np.float32) if c == 0
                else np.ascontiguousarray(h[c * 1024 - 3:c * 1024]))
        asl = np.ascontiguousarray(
            Alog[c * 512:(c + 1) * 512].reshape(4, 128, 16)
            .transpose(1, 0, 2).reshape(128, 64))
        dsl = np.ascontiguousarray(Dv[c * 512:(c + 1) * 512].reshape(4, 128).T)
        in_maps.append(dict(
            h_main=np.ascontiguousarray(h[c * 1024:(c + 1) * 1024]),
            h_halo=halo,
            winT=winT_h, woutT=woutT_h, wgT=wgT_h, wuT=wuT_h,
            wdnT=wdnT_h, wx_in=wx_h, wdt_in=wdt_h,
            a_sl=asl, d_sl=dsl, convw=convw_h, convb=convb_h, bdt=bdt_h,
            bout=bout_h, ones1=ones_h, ident=id_h,
        ))

    if "nc" not in _NC_CACHE:
        _NC_CACHE["nc"] = build_program()
    nc = _NC_CACHE["nc"]
    res = run_bass_kernel_spmd(nc, in_maps, core_ids=list(range(NCORES)))
    out = np.concatenate([res.results[c]["o"] for c in range(NCORES)], axis=0)
    return out.reshape(1, 8192, 2048).astype(np.float32)


# version 30
# speedup vs baseline: 1.1078x; 1.1078x over previous
import contextlib
import os
import numpy as np

os.environ.setdefault("TRN_ALLOW_ICI_ALL_TO_ALL", "1")

import concourse.bass as bass
import concourse.mybir as mybir
from concourse.bass_utils import run_bass_kernel_spmd

F32 = mybir.dt.float32
BF16 = mybir.dt.bfloat16
AF = mybir.ActivationFunctionType
ALU = mybir.AluOpType
AX = mybir.AxisListType

NCORES = 8
T = 1024
D = 2048
DI = 4096
NST = 16
RNK = 128
HF = 8192
S = 512
NT = 4
W = 256
NW = 8192 // W
EPS = 1e-5
RG = [list(range(NCORES))]


def build_program():
    nc = bass.Bass("TRN2", target_bir_lowering=False, debug=False)

    def ein(name, shape, dt=F32):
        return nc.dram_tensor(name, shape, dt, kind="ExternalInput").ap()

    h_main = ein("h_main", [T, D])
    h_halo = ein("h_halo", [3, D])
    winT = ein("winT", [D, HF])
    woutT = ein("woutT", [DI, D])
    wgT = ein("wgT", [D, HF])
    wuT = ein("wuT", [D, HF])
    wdnT = ein("wdnT", [HF, D])
    wx_in = ein("wx_in", [128, 32 * 160])
    wdt_in = ein("wdt_in", [128, DI])
    a_sl = ein("a_sl", [128, NT * NST])
    d_sl = ein("d_sl", [128, NT])
    convw = ein("convw", [128, 32 * 4])
    convb = ein("convb", [128, 32])
    bdt = ein("bdt", [128, 32])
    bout = ein("bout", [1, D])
    ones1 = ein("ones1", [1, 128])
    ident = ein("ident", [128, 128])
    out_o = nc.dram_tensor("o", [T, D], F32, kind="ExternalOutput").ap()

    def idr(name, shape, dt=F32):
        return nc.dram_tensor(name, shape, dt)

    def idrs(name, shape, dt=F32):
        return nc.dram_tensor(name, shape, dt, addr_space="Shared")

    wxTb = idr("wxTb", [128, 32 * 160], BF16)
    wdtTb = idr("wdtTb", [128, DI], BF16)
    winTb = idr("winTb", [D, HF], BF16)
    wgTb = idr("wgTb", [D, HF], BF16)
    wuTb = idr("wuTb", [D, HF], BF16)
    woutTc = idr("woutTc", [DI, D], BF16)
    wdnTc = idr("wdnTc", [HF, D], BF16)
    z_buf = idr("z_buf", [DI, T])
    g_buf = idr("g_buf", [HF, T], BF16)
    u_buf = idr("u_buf", [HF, T], BF16)
    ffn_buf = idr("ffn_buf", [T, D])
    du_in = idr("du_in", [2 * DI, T])
    du_out = idr("du_out", [2 * DI, T])
    bc_in = idr("bc_in", [256, 128])
    bc_all = idr("bc_all", [NCORES * 256, 128])
    y_in = idr("y_in", [DI, T])
    y_loc = idr("y_loc", [DI, T])
    fence_in = idr("fence_in", [1, 8])
    fence_out = [idr(f"fence_out{i}", [NCORES, 8]) for i in range(2)]

    with contextlib.ExitStack() as ctx:
        def sbt(name, shape, dt=F32):
            return ctx.enter_context(nc.sbuf_tensor(name, shape, dt))

        sbA = sbt("sbA", [128, 64])
        sbAe = sbt("sbAe", [128, 64])
        sbAn = sbt("sbAn", [128, 64])
        sbD = sbt("sbD", [128, NT])
        sbCW = sbt("sbCW", [128, 128])
        sbCB = sbt("sbCB", [128, 32])
        sbBDT = sbt("sbBDT", [128, 32])
        sbONES = sbt("sbONES", [1, 128])
        sbID = sbt("sbID", [128, 128])
        sbSTAT = sbt("sbSTAT", [128, 16])

        AFSZ = 52400
        arena = sbt("arena", [128, AFSZ])
        arv = arena[:]

        def rf(off, size):
            assert off + size <= AFSZ, (off, size)
            return arv[:, off:off + size]

        def rbv(off, size_f32):
            return rf(off, size_f32).bitcast(BF16)

        XNT = rbv(0, 8216)

        def xnt(dk):
            return XNT[:, dk * 1027:(dk + 1) * 1027]

        WTB = [rbv(8216, 2048), rbv(10264, 2048)]
        RA = 12320

        A_xn = rf(RA + 0, 2048)
        A_h = rf(RA + 2048, 2048)
        A_scr = rf(RA + 4096, 2048)
        A_xt = [rf(RA + 6144, 1028), rf(RA + 7172, 1028)]
        A_hst = rf(RA + 8200, 512)
        A_trs = rf(RA + 8712, 32)
        A_bc = rf(RA + 8744, 1024)
        A_ct = rf(RA + 9768, 128)
        A_dl = [rf(RA + 9896, 1024), rf(RA + 10920, 1024)]
        A_sp = [rf(RA + 11944 + i * 512, 512) for i in range(3)]
        A_cacc = [rf(RA + 13480, 1024), rf(RA + 14504, 1024)]
        A_xcf = [rf(RA + 15528, 1024), rf(RA + 16552, 1024)]
        A_dtp = rbv(RA + 17576, 512)
        A_wxT = rbv(RA + 18088, 2560)
        A_wdtT = rbv(RA + 20648, 2048)
        A_xcb = rbv(RA + 22696, 16384)

        S_s = [rf(RA + 0, 4096), rf(RA + 0, 4096)]
        S_a = [rf(RA + 8192, 4096), rf(RA + 4096, 4096)]
        S_b = rf(RA + 12288, 4096)
        S_Br = rf(RA + 16384, 4096)
        S_Cr = rf(RA + 20480, 4096)
        S_dsl = [rf(RA + 24576, 1024), rf(RA + 25600, 1024)]
        S_usl = rf(RA + 26624, 1024)
        S_du = rf(RA + 27648, 1024)
        S_y = rf(RA + 28672, 1024)
        S_carry = rf(RA + 29696, 64)
        RFN = RA + 29760

        F_z = rf(RFN + 0, 1024)
        Fg = [[rbv(RFN + 4096 + w2 * 1024 + q * 256, 256) for q in range(4)]
              for w2 in range(2)]
        Fu = [[rbv(RFN + 6144 + w2 * 1024 + q * 256, 256) for q in range(4)]
              for w2 in range(2)]
        Fg_ld = [rbv(RFN + 0, 1024), rbv(RFN + 1024, 1024)]
        WTBg2 = [rbv(RFN + 0, 2048), rbv(RFN + 2048, 2048)]
        Fu_ld = [rbv(RFN + 2048, 1024), rbv(RFN + 3072, 1024)]
        F_glt = [rbv(RFN + 4096, 1024), rbv(RFN + 5120, 1024)]
        F_dstr = [rbv(RFN + 6144, 1024), rbv(RFN + 7168, 1024)]
        F_ffn = [rf(RFN + 8192 + i * 512, 512) for i in range(2)]

        P5_zs = [rf(RA + 0, 1024), rf(RA + 1024, 1024)]
        P5_y = [rf(RA + 2048, 1024), rf(RA + 3072, 1024)]
        P5_hb = rf(RA + 4096, 2048)
        P5_ffn = rf(RA + 6144, 2048)
        P5_brep = rf(RA + 8192, 2048)
        P5_out = [rf(RA + 10240, 2048), rf(RA + 12288, 2048)]
        P5_y2 = rbv(RA + 14336, 16384)
        P5_wo = [rbv(RA + 30720, 1024), rbv(RA + 31744, 1024)]

        P = [ctx.enter_context(nc.psum_tensor(f"ps{i}", [128, 512], F32))
             for i in range(8)]

        sems, cnt, waited = {}, {}, {}
        for s in ["sy", "gd", "ad", "cc", "pe", "act", "dve", "po"]:
            sems[s] = ctx.enter_context(nc.semaphore(s))
            cnt[s] = 0

        def inc(instr, s, k=1):
            instr.then_inc(sems[s], k)
            cnt[s] += k
            return cnt[s]

        def wt(eng, en, s, v):
            if not v or v <= 0:
                return
            wd = waited.setdefault(en, {})
            if wd.get(s, 0) >= v:
                return
            eng.wait_ge(sems[s], v)
            wd[s] = v

        block = ctx.enter_context(nc.Block())

        def sec(en, fn):
            getattr(block, en)(fn)

        tok = {}
        bank = [None] * 8

        def bank_wait(eng, en, i):
            if bank[i] is not None:
                wt(eng, en, bank[i][0], bank[i][1])

        def ph0_sy(sy):
            for dst, src in ((sbA, a_sl), (sbD, d_sl), (sbCW, convw),
                             (sbCB, convb), (sbBDT, bdt),
                             (sbONES, ones1), (sbID, ident)):
                inc(sy.dma_start(out=dst[:], in_=src), "sy", 16)
            tok["params"] = cnt["sy"]
        sec("sync", ph0_sy)

        def ph0_po(po):
            inc(po.dma_start(out=winTb.ap()[:, 0:DI], in_=winT[:, 0:DI]),
                "gd", 16)
            tok[("staged", "winx")] = cnt["gd"]
            wt(po, "po", "gd", cnt["gd"])
            for name, dst, src_ in (("wx", wxTb.ap(), wx_in),
                                    ("wdt", wdtTb.ap(), wdt_in),
                                    ("winz", winTb.ap()[:, DI:HF],
                                     winT[:, DI:HF])):
                inc(po.dma_start(out=dst, in_=src_), "gd", 16)
                tok[("staged", name)] = cnt["gd"]
                wt(po, "po", "gd", cnt["gd"])
            c0, c1 = 0, 2048
            inc(po.dma_start(out=wgTb.ap()[:, c0:c1],
                             in_=wgT[:, c0:c1]), "gd", 16)
            tok[("staged", "wg", 0)] = cnt["gd"]
            wt(po, "po", "gd", cnt["gd"])
            inc(po.dma_start(out=wuTb.ap()[:, c0:c1],
                             in_=wuT[:, c0:c1]), "gd", 16)
            tok[("staged", "wu", 0)] = cnt["gd"]
            wt(po, "po", "gd", cnt["gd"])
        sec("gpsimd", ph0_po)

        def ph0_sc_a(sc):
            wt(sc, "sc", "sy", tok["params"])
            inc(sc.activation(out=sbAe[:], in_=sbA[:], func=AF.Exp), "act")
            tok["a_exp"] = cnt["act"]
        sec("scalar", ph0_sc_a)

        def ph0_ve_a(ve):
            wt(ve, "ve", "act", tok["a_exp"])
            inc(ve.tensor_scalar_mul(sbAn[:], sbAe[:], -1.0), "dve")
            tok["a_neg"] = cnt["dve"]
        sec("vector", ph0_ve_a)

        for i in range(9):
            lt = i - 1
            if i == 0:
                def f_ms(ve):
                    inc(ve.memset(A_h[:], 0.0), "dve")
                    tok["memset_h"] = cnt["dve"]
                sec("vector", f_ms)

            def f_sy(sy, i=i, lt=lt):
                if i == 0:
                    wt(sy, "sy", "dve", tok["memset_h"])
                    inc(sy.dma_start(out=A_h[0:3, :], in_=h_halo), "sy", 16)
                else:
                    wt(sy, "sy", "dve", tok.get(("xn", i - 1), 0))
                    inc(sy.dma_start(out=A_h[:],
                                     in_=h_main[lt * 128:(lt + 1) * 128, :]),
                        "sy", 16)
                tok[("hload", i)] = cnt["sy"]
            sec("sync", f_sy)

            def f_sq(sc, i=i):
                wt(sc, "sc", "sy", tok[("hload", i)])
                wt(sc, "sc", "dve", tok.get(("lnstat", i - 1), 0))
                inc(sc.activation(out=A_scr[:], in_=A_h[:], func=AF.Square),
                    "act")
                tok[("hsq", i)] = cnt["act"]
            sec("scalar", f_sq)

            def f_ve(ve, i=i):
                wt(ve, "ve", "sy", tok[("hload", i)])
                wt(ve, "ve", "act", tok.get(("rstd", i - 1), 0))
                ve.tensor_reduce(out=sbSTAT[:, 0:1], in_=A_h[:], axis=AX.X,
                                 op=ALU.add)
                wt(ve, "ve", "act", tok[("hsq", i)])
                v0 = inc(ve.tensor_reduce(out=sbSTAT[:, 1:2], in_=A_scr[:],
                                          axis=AX.X, op=ALU.add), "dve")
                wt(ve, "ve", "dve", v0)
                v1 = inc(ve.tensor_scalar_mul(sbSTAT[:, 3:4], sbSTAT[:, 0:1],
                                              -1.0 / 2048.0), "dve")
                wt(ve, "ve", "dve", v1)
                v2 = inc(ve.tensor_mul(sbSTAT[:, 4:5], sbSTAT[:, 3:4],
                                       sbSTAT[:, 3:4]), "dve")
                v3 = inc(ve.tensor_scalar_mul(sbSTAT[:, 5:6], sbSTAT[:, 1:2],
                                              1.0 / 2048.0), "dve")
                wt(ve, "ve", "dve", v3)
                v4 = inc(ve.tensor_sub(sbSTAT[:, 6:7], sbSTAT[:, 5:6],
                                       sbSTAT[:, 4:5]), "dve")
                wt(ve, "ve", "dve", v4)
                inc(ve.tensor_scalar_add(sbSTAT[:, 6:7], sbSTAT[:, 6:7],
                                         EPS), "dve")
                tok[("lnstat", i)] = cnt["dve"]
            sec("vector", f_ve)

            def f_sc(sc, i=i):
                wt(sc, "sc", "dve", tok[("lnstat", i)])
                a0 = inc(sc.activation(out=sbSTAT[:, 8:9],
                                       in_=sbSTAT[:, 6:7], func=AF.Ln),
                         "act")
                wt(sc, "sc", "act", a0)
                inc(sc.activation(out=sbSTAT[:, 7:8], in_=sbSTAT[:, 8:9],
                                  func=AF.Exp, scale=-0.5), "act")
                tok[("rstd", i)] = cnt["act"]
            sec("scalar", f_sc)

            def f_ve2(ve, i=i):
                wt(ve, "ve", "act", tok[("rstd", i)])
                wt(ve, "ve", "pe", tok.get(("tp", i - 1, 15), 0))
                inc(ve.tensor_scalar(out=A_xn[:], in0=A_h[:],
                                     scalar1=sbSTAT[:, 3:4],
                                     scalar2=sbSTAT[:, 7:8],
                                     op0=ALU.add, op1=ALU.mult), "dve")
                tok[("xn", i)] = cnt["dve"]
            sec("vector", f_ve2)

            for dk in range(16):
                b = 4 + dk % 2

                def f_te(te, i=i, dk=dk, b=b):
                    wt(te, "te", "sy", tok["params"])
                    wt(te, "te", "dve", tok[("xn", i)])
                    bank_wait(te, "te", b)
                    m = te.transpose(P[b][:, 0:128],
                                     in_=A_xn[:, dk * 128:(dk + 1) * 128],
                                     identity=sbID[:])
                    inc(m, "pe")
                    tok[("tp", i, dk)] = cnt["pe"]
                sec("tensor", f_te)

                def f_sc2(sc, i=i, dk=dk, b=b, lt=lt):
                    wt(sc, "sc", "pe", tok[("tp", i, dk)])
                    if i == 0:
                        o, src = xnt(dk)[:, 0:3], P[b][:, 0:3]
                    else:
                        o = xnt(dk)[:, 3 + lt * 128:3 + (lt + 1) * 128]
                        src = P[b][:, 0:128]
                    v = inc(sc.activation(out=o, in_=src, func=AF.Copy), "act")
                    bank[b] = ("act", v)
                sec("scalar", f_sc2)
        XNT_done = cnt["act"]

        def f_wx(sy):
            wt(sy, "sy", "gd", tok[("staged", "wx")])
            inc(sy.dma_start(out=A_wxT[:, 0:5120], in_=wxTb.ap()), "sy", 16)
            tok["wxT"] = cnt["sy"]
            wt(sy, "sy", "gd", tok[("staged", "wdt")])
            inc(sy.dma_start(out=A_wdtT[:, 0:4096], in_=wdtTb.ap()),
                "sy", 16)
            tok["wdtT"] = cnt["sy"]
        sec("sync", f_wx)



        wt_lastmm = [0, 0]

        def win_block_ap(src_all, e0):
            return bass.AP(src_all, e0,
                           [[HF, 128], [128 * HF, 16], [1, 256]])

        for ms in range(16):
            par = ms % 2
            mg = [2 * ms, 2 * ms + 1]

            def f_sy(sy, ms=ms, par=par):
                wt(sy, "sy", "gd", tok[("staged", "winx")])
                wt(sy, "sy", "pe", wt_lastmm[par])
                inc(sy.dma_start(out=WTB[par][:, 0:4096],
                                 in_=win_block_ap(winTb, ms * 256)), "sy", 16)
                tok[("wtx", ms)] = cnt["sy"]
            sec("sync", f_sy)

            def f_te(te, ms=ms, par=par):
                wt(te, "te", "sy", tok[("wtx", ms)])
                wt(te, "te", "act", XNT_done)
                for b in range(4):
                    bank_wait(te, "te", b)
                bank_wait(te, "te", 4)
                m = None
                for k in range(16):
                    for j in range(2):
                        lhs = WTB[par][:, k * 256 + j * 128:
                                       k * 256 + (j + 1) * 128]
                        te.matmul(P[2 * j][:], lhsT=lhs,
                                  rhs=xnt(k)[:, 3:515],
                                  start=(k == 0), stop=(k == 15))
                        m = te.matmul(P[2 * j + 1][:], lhsT=lhs,
                                      rhs=xnt(k)[:, 515:1027],
                                      start=(k == 0), stop=(k == 15))
                inc(m, "pe")
                tok[("xmm", ms)] = cnt["pe"]
                for k in range(16):
                    m = te.matmul(P[4][0:3, 0:256], lhsT=xnt(k)[:, 0:3],
                                  rhs=WTB[par][:, k * 256:(k + 1) * 256],
                                  start=(k == 0), stop=(k == 15))
                inc(m, "pe")
                tok[("hmm", ms)] = cnt["pe"]
                wt_lastmm[par] = cnt["pe"]
            sec("tensor", f_te)

            def f_sc(sc, ms=ms):
                wt(sc, "sc", "pe", tok[("xmm", ms)])
                for j in range(2):
                    wt(sc, "sc", "dve", tok.get(("conv", ms - 1, j), 0))
                    v = inc(sc.activation(out=A_xt[j][:, 3:515],
                                          in_=P[2 * j][:], func=AF.Copy),
                            "act")
                    bank[2 * j] = ("act", v)
                    v = inc(sc.activation(out=A_xt[j][:, 515:1027],
                                          in_=P[2 * j + 1][:], func=AF.Copy),
                            "act")
                    bank[2 * j + 1] = ("act", v)
                wt(sc, "sc", "pe", tok[("hmm", ms)])
                v = inc(sc.activation(out=A_hst[0:3, 0:256],
                                      in_=P[4][0:3, 0:256], func=AF.Copy),
                        "act")
                bank[4] = ("act", v)
                tok[("xev", ms)] = cnt["act"]
            sec("scalar", f_sc)

            def f_ve(ve, ms=ms, mg=mg):
                wt(ve, "ve", "act", tok[("xev", ms)])
                for blk in range(8):
                    ve.transpose(A_scr[0:32, blk * 32:(blk + 1) * 32],
                                 A_hst[0:32, blk * 32:(blk + 1) * 32])
                for blk in range(8):
                    j = blk // 4
                    dbase = (blk % 4) * 32
                    ve.tensor_copy(A_xt[j][dbase:dbase + 32, 0:3],
                                   A_scr[0:32, blk * 32 :blk * 32 + 3])
                for j in range(2):
                    m = mg[j]
                    ve.tensor_scalar_mul(A_cacc[j][:], A_xt[j][:, 0:1024],
                                         sbCW[:, 4 * m:4 * m + 1])
                    last = None
                    for kk in range(1, 4):
                        last = ve.scalar_tensor_tensor(
                            out=A_cacc[j][:], in0=A_xt[j][:, kk:kk + 1024],
                            scalar=sbCW[:, 4 * m + kk:4 * m + kk + 1],
                            in1=A_cacc[j][:], op0=ALU.mult, op1=ALU.add)
                    inc(last, "dve")
                    tok[("conv", ms, j)] = cnt["dve"]
            sec("vector", f_ve)

            def f_sc2(sc, ms=ms, mg=mg):
                for j in range(2):
                    m = mg[j]
                    wt(sc, "sc", "dve", tok[("conv", ms, j)])
                    wt(sc, "sc", "sy", tok.get(("ustore", ms - 1, j), 0))
                    wt(sc, "sc", "dve", tok.get(("xcb", ms - 1, j), 0))
                    inc(sc.activation(out=A_xcf[j][:], in_=A_cacc[j][:],
                                      func=AF.Silu, bias=sbCB[:, m:m + 1]),
                        "act")
                    tok[("xcf", ms, j)] = cnt["act"]
            sec("scalar", f_sc2)

            def f_sy2(sy, ms=ms, mg=mg):
                for j in range(2):
                    m = mg[j]
                    wt(sy, "sy", "act", tok[("xcf", ms, j)])
                    r0 = (m // 4) * 1024 + 512 + (m % 4) * 128
                    inc(sy.dma_start(
                        out=du_in.ap()[r0:r0 + 128, :],
                        in_=A_xcf[j][:]), "sy", 16)
                    tok[("ustore", ms, j)] = cnt["sy"]
            sec("sync", f_sy2)

            def f_ve2(ve, ms=ms, mg=mg):
                for j in range(2):
                    m = mg[j]
                    wt(ve, "ve", "act", tok[("xcf", ms, j)])
                    inc(ve.tensor_copy(A_xcb[:, m * 1024:(m + 1) * 1024],
                                       A_xcf[j][:]), "dve")
                    tok[("xcb", ms, j)] = cnt["dve"]
            sec("vector", f_ve2)
        xcb_done = cnt["dve"]

        def f_wx_te(te):
            wt(te, "te", "sy", tok["wxT"])
            wt(te, "te", "dve", xcb_done)
            for b in range(4):
                bank_wait(te, "te", b)
            m = None
            for k in range(32):
                lhs0 = A_wxT[:, k * 160:k * 160 + 128]
                lhs1 = A_wxT[:, k * 160 + 128:(k + 1) * 160]
                mv0 = A_xcb[:, k * 1024:k * 1024 + 512]
                mv1 = A_xcb[:, k * 1024 + 512:(k + 1) * 1024]
                te.matmul(P[0][:], lhsT=lhs0, rhs=mv0, start=(k == 0),
                          stop=(k == 31))
                te.matmul(P[1][:], lhsT=lhs0, rhs=mv1, start=(k == 0),
                          stop=(k == 31))
                te.matmul(P[2][0:32, :], lhsT=lhs1, rhs=mv0, start=(k == 0),
                          stop=(k == 31))
                m = te.matmul(P[3][0:32, :], lhsT=lhs1, rhs=mv1,
                              start=(k == 0), stop=(k == 31))
            inc(m, "pe")
            tok["wx_mm"] = cnt["pe"]
        sec("tensor", f_wx_te)

        def f_wx_sc(sc):
            wt(sc, "sc", "pe", tok["wx_mm"])
            v = inc(sc.activation(out=A_dtp[:, 0:512], in_=P[0][:],
                                  func=AF.Copy), "act")
            bank[0] = ("act", v)
            v = inc(sc.activation(out=A_dtp[:, 512:1024], in_=P[1][:],
                                  func=AF.Copy), "act")
            bank[1] = ("act", v)
            v = inc(sc.activation(out=A_bc[0:32, 0:512], in_=P[2][0:32, :],
                                  func=AF.Copy), "act")
            bank[2] = ("act", v)
            v = inc(sc.activation(out=A_bc[0:32, 512:1024],
                                  in_=P[3][0:32, :], func=AF.Copy), "act")
            bank[3] = ("act", v)
            tok["bc_ev"] = cnt["act"]
        sec("scalar", f_wx_sc)

        def f_b_store(sy):
            wt(sy, "sy", "act", tok["bc_ev"])
            o = bass.AP(bc_in, 0, [[1024, 16], [1, 1024]])
            inc(sy.dma_start(out=o, in_=A_bc[0:16, :]), "sy", 16)
        sec("sync", f_b_store)

        for c8 in range(8):
            def f_t(te, c8=c8):
                wt(te, "te", "act", tok["bc_ev"])
                bank_wait(te, "te", 4)
                m = te.transpose(P[4][:, 0:32],
                                 in_=A_bc[0:32, c8 * 128:(c8 + 1) * 128],
                                 identity=sbID[0:32, 0:32])
                inc(m, "pe")
                tok[("ctp", c8)] = cnt["pe"]
            sec("tensor", f_t)

            def f_s(sc, c8=c8):
                wt(sc, "sc", "pe", tok[("ctp", c8)])
                v = inc(sc.activation(out=A_ct[:, c8 * 16:(c8 + 1) * 16],
                                      in_=P[4][:, 16:32], func=AF.Copy),
                        "act")
                bank[4] = ("act", v)
            sec("scalar", f_s)
        tok["ct_ev"] = cnt["act"]

        def f_c_store(sy):
            wt(sy, "sy", "act", tok["ct_ev"])
            o = bass.AP(bc_in, 16384, [[16, 128], [2048, 8], [1, 16]])
            inc(sy.dma_start(out=o, in_=A_ct[:]), "sy", 16)
        sec("sync", f_c_store)

        last_dl = [0]
        for m in range(32):
            mr = m % 2

            def f_t(te, m=m):
                wt(te, "te", "sy", tok["wdtT"])
                wt(te, "te", "act", tok["bc_ev"])
                bank_wait(te, "te", 5)
                bank_wait(te, "te", 6)
                lhs = A_wdtT[:, m * 128:(m + 1) * 128]
                te.matmul(P[5][:], lhsT=lhs, rhs=A_dtp[:, 0:512],
                          start=True, stop=True)
                m2 = te.matmul(P[6][:], lhsT=lhs, rhs=A_dtp[:, 512:1024],
                               start=True, stop=True)
                inc(m2, "pe")
                tok[("dtmm", m)] = cnt["pe"]
            sec("tensor", f_t)

            def f_s(sc, m=m):
                wt(sc, "sc", "pe", tok[("dtmm", m)])
                wt(sc, "sc", "dve", last_dl[0])
                bcol = sbBDT[:, m:m + 1]
                SP0 = [A_sp[0][:], A_cacc[0][:, 0:512]]
                SP1 = [A_sp[1][:], A_cacc[0][:, 512:1024]]
                SP2 = [A_sp[2][:], A_cacc[1][:, 0:512]]
                for ch in range(2):
                    sc.activation(out=SP0[ch], in_=P[5 + ch][:],
                                  func=AF.Relu, bias=bcol)
                for ch in range(2):
                    sc.activation(out=SP1[ch], in_=P[5 + ch][:],
                                  func=AF.Abs, bias=bcol)
                for ch in range(2):
                    sc.activation(out=SP2[ch], in_=SP1[ch],
                                  func=AF.Exp, scale=-1.0)
                v = None
                for ch in range(2):
                    v = inc(sc.activation(out=SP1[ch], in_=SP2[ch],
                                          func=AF.Ln, bias=1.0), "act")
                bank[5] = ("act", v)
                bank[6] = ("act", v)
                tok[("spl", m)] = cnt["act"]
            sec("scalar", f_s)

            def f_v(ve, m=m, mr=mr):
                SP0 = [A_sp[0][:], A_cacc[0][:, 0:512]]
                SP1 = [A_sp[1][:], A_cacc[0][:, 512:1024]]
                wt(ve, "ve", "act", tok[("spl", m)])
                wt(ve, "ve", "sy", tok.get(("dstore", m - 2), 0))
                for ch in range(2):
                    inc(ve.tensor_add(A_dl[mr][:, ch * 512:(ch + 1) * 512],
                                      SP1[ch], SP0[ch]), "dve")
                tok[("dl", m, 1)] = cnt["dve"]
                last_dl[0] = cnt["dve"]
            sec("vector", f_v)

            def f_y(sy, m=m, mr=mr):
                wt(sy, "sy", "dve", tok[("dl", m, 1)])
                r0 = (m // 4) * 1024 + (m % 4) * 128
                inc(sy.dma_start(out=du_in.ap()[r0:r0 + 128, :],
                                 in_=A_dl[mr][:]), "sy", 16)
                tok[("dstore", m)] = cnt["sy"]
            sec("sync", f_y)
        tok["du_done"] = cnt["sy"]

        def f_agdu(po):
            wt(po, "po", "sy", tok["du_done"])
            wt(po, "po", "gd", tok[("staged", "wu", 0)])
            m = po.collective_compute("AllToAll", ALU.bypass,
                                      replica_groups=RG,
                                      ins=[du_in.ap()], outs=[du_out.ap()])
            inc(m, "cc")
            m = po.collective_compute("AllGather", ALU.bypass,
                                      replica_groups=RG,
                                      ins=[bc_in.ap()], outs=[bc_all.ap()])
            inc(m, "cc")
            wt(po, "po", "cc", cnt["cc"])
            m = po.collective_compute("AllGather", ALU.bypass,
                                      replica_groups=RG,
                                      ins=[fence_in.ap()],
                                      outs=[fence_out[0].ap()])
            inc(m, "cc")
            tok["cc_du"] = cnt["cc"]
            tok["cc_bc"] = cnt["cc"]
        sec("gpsimd", f_agdu)

        for ms in range(16):
            par = ms % 2

            def f_sy(sy, ms=ms, par=par):
                wt(sy, "sy", "gd", tok[("staged", "winz")])
                wt(sy, "sy", "pe", wt_lastmm[par])
                inc(sy.dma_start(out=WTB[par][:, 0:4096],
                                 in_=win_block_ap(winTb, 4096 + ms * 256)),
                    "sy", 16)
                tok[("wtz", ms)] = cnt["sy"]
            sec("sync", f_sy)

            def f_te(te, ms=ms, par=par):
                wt(te, "te", "sy", tok[("wtz", ms)])
                for b in range(4):
                    bank_wait(te, "te", b)
                m = None
                for k in range(16):
                    for j in range(2):
                        lhs = WTB[par][:, k * 256 + j * 128:
                                       k * 256 + (j + 1) * 128]
                        te.matmul(P[2 * j][:], lhsT=lhs,
                                  rhs=xnt(k)[:, 3:515],
                                  start=(k == 0), stop=(k == 15))
                        m = te.matmul(P[2 * j + 1][:], lhsT=lhs,
                                      rhs=xnt(k)[:, 515:1027],
                                      start=(k == 0), stop=(k == 15))
                inc(m, "pe")
                tok[("zmm", ms)] = cnt["pe"]
                wt_lastmm[par] = cnt["pe"]
            sec("tensor", f_te)

            for j in range(2):
                def f_sc(sc, ms=ms, j=j):
                    wt(sc, "sc", "pe", tok[("zmm", ms)])
                    wt(sc, "sc", "sy", tok.get(("zstore", ms, j - 1),
                                               tok.get(("zstore", ms - 1, 1),
                                                       0)))
                    v = inc(sc.activation(out=F_z[:, 0:512],
                                          in_=P[2 * j][:], func=AF.Copy),
                            "act")
                    bank[2 * j] = ("act", v)
                    v = inc(sc.activation(out=F_z[:, 512:1024],
                                          in_=P[2 * j + 1][:], func=AF.Copy),
                            "act")
                    bank[2 * j + 1] = ("act", v)
                    tok[("zev", ms, j)] = cnt["act"]
                sec("scalar", f_sc)

                def f_sy2(sy, ms=ms, j=j):
                    wt(sy, "sy", "act", tok[("zev", ms, j)])
                    mz = 2 * ms + j
                    inc(sy.dma_start(
                        out=z_buf.ap()[mz * 128:(mz + 1) * 128, :],
                        in_=F_z[:]), "sy", 16)
                    tok[("zstore", ms, j)] = cnt["sy"]
                sec("sync", f_sy2)

        zlast = (wt_lastmm[0], wt_lastmm[1])

        stok = {}
        gu_tok = {}

        def gu_ms(ms):
            gt = [WTB[0], WTB[1]] if ms % 2 == 0 else [WTBg2[0], WTBg2[1]]

            def f_sy(sy, ms=ms, gt=gt):
                wt(sy, "sy", "gd", tok[("staged", "wg", ms // 8)])
                wt(sy, "sy", "pe", gu_tok.get(("gmm", ms - 2), zlast[0]))
                inc(sy.dma_start(out=gt[0][:, 0:4096],
                                 in_=win_block_ap(wgTb, ms * 256)), "sy", 16)
                gu_tok[("wg", ms)] = cnt["sy"]
                wt(sy, "sy", "gd", tok[("staged", "wu", ms // 8)])
                wt(sy, "sy", "pe", gu_tok.get(("umm", ms - 2), zlast[1]))
                inc(sy.dma_start(out=gt[1][:, 0:4096],
                                 in_=win_block_ap(wuTb, ms * 256)), "sy", 16)
                gu_tok[("wu", ms)] = cnt["sy"]
            sec("sync", f_sy)

            def f_te(te, ms=ms, gt=gt):
                wt(te, "te", "sy", gu_tok[("wg", ms)])
                for b in range(8):
                    bank_wait(te, "te", b)
                m = None
                for k in range(16):
                    for j in range(2):
                        lhs = gt[0][:, k * 256 + j * 128:
                                    k * 256 + (j + 1) * 128]
                        te.matmul(P[2 * j][:], lhsT=lhs,
                                  rhs=xnt(k)[:, 3:515],
                                  start=(k == 0), stop=(k == 15))
                        m = te.matmul(P[2 * j + 1][:], lhsT=lhs,
                                      rhs=xnt(k)[:, 515:1027],
                                      start=(k == 0), stop=(k == 15))
                inc(m, "pe")
                gu_tok[("gmm", ms)] = cnt["pe"]
                wt(te, "te", "sy", gu_tok[("wu", ms)])
                for k in range(16):
                    for j in range(2):
                        lhs = gt[1][:, k * 256 + j * 128:
                                    k * 256 + (j + 1) * 128]
                        te.matmul(P[4 + 2 * j][:], lhsT=lhs,
                                  rhs=xnt(k)[:, 3:515],
                                  start=(k == 0), stop=(k == 15))
                        m = te.matmul(P[5 + 2 * j][:], lhsT=lhs,
                                      rhs=xnt(k)[:, 515:1027],
                                      start=(k == 0), stop=(k == 15))
                inc(m, "pe")
                gu_tok[("umm", ms)] = cnt["pe"]
            sec("tensor", f_te)

            def f_sc(sc, ms=ms):
                w2 = ms % 2
                wt(sc, "sc", "pe", gu_tok[("gmm", ms)])
                wt(sc, "sc", "gd", gu_tok.get(("gst_store", ms - 2), 0))
                for q in range(4):
                    v = inc(sc.activation(out=Fg[w2][q][:], in_=P[q][:],
                                          func=AF.Silu), "act")
                    bank[q] = ("act", v)
                wt(sc, "sc", "pe", gu_tok[("umm", ms)])
                for q in range(4):
                    v = inc(sc.activation(out=Fu[w2][q][:], in_=P[4 + q][:],
                                          func=AF.Copy), "act")
                    bank[4 + q] = ("act", v)
                gu_tok[("guev", ms)] = cnt["act"]
            sec("scalar", f_sc)

            def f_sy2(po, ms=ms):
                w2 = ms % 2
                wt(po, "po", "act", gu_tok[("guev", ms)])
                for j in range(2):
                    r0 = (2 * ms + j) * 128
                    for jj in range(2):
                        q = 2 * j + jj
                        inc(po.dma_start(
                            out=g_buf.ap()[r0:r0 + 128,
                                           jj * 512:(jj + 1) * 512],
                            in_=Fg[w2][q][:]), "gd", 16)
                        inc(po.dma_start(
                            out=u_buf.ap()[r0:r0 + 128,
                                           jj * 512:(jj + 1) * 512],
                            in_=Fu[w2][q][:]), "gd", 16)
                wt(po, "po", "gd", cnt["gd"])
                gu_tok[("gst_store", ms)] = cnt["gd"]
            sec("gpsimd", f_sy2)

        def scan_w(w):
            par = w % 2

            def f_ld(sy, w=w, par=par):
                rb_i = w // 4
                l0 = w * W - rb_i * 1024
                wt(sy, "sy", "cc", tok["cc_du"])
                wt(sy, "sy", "act", stok.get(("aexp_done", w - 2), 0))
                duo = rb_i * (1024 * T) + l0
                inc(sy.dma_start(
                    out=S_dsl[par][:],
                    in_=bass.AP(du_out, duo,
                                [[T, 128], [128 * T, 4], [1, W]])),
                    "sy", 16)
                stok[("dsl", w)] = cnt["sy"]
                wt(sy, "sy", "dve", stok.get(("stt_done", w - 1), 0))
                inc(sy.dma_start(
                    out=S_usl[:],
                    in_=bass.AP(du_out, duo + 512 * T,
                                [[T, 128], [128 * T, 4], [1, W]])),
                    "sy", 16)
                stok[("usl", w)] = cnt["sy"]
                wt(sy, "sy", "cc", tok["cc_bc"])
                wt(sy, "sy", "dve", stok.get(("b_done", w - 1), 0))
                offb = rb_i * 256 * 128
                inc(sy.dma_start(
                    out=S_Br[:],
                    in_=bass.AP(bc_all, offb + l0,
                                [[0, 128], [1024, 16], [1, W]])), "sy", 16)
                stok[("Br", w)] = cnt["sy"]
                wt(sy, "sy", "dve", stok.get(("cm_done", w - 1), 0))
                inc(sy.dma_start(
                    out=S_Cr[:],
                    in_=bass.AP(bc_all, offb + 16384 + l0 * 16,
                                [[0, 128], [16, W], [1, 16]])), "sy", 16)
                stok[("Cr", w)] = cnt["sy"]
            sec("sync", f_ld)

            def f_du(ve, w=w, par=par):
                wt(ve, "ve", "sy", stok[("usl", w)])
                inc(ve.tensor_mul(S_du[:], S_dsl[par][:], S_usl[:]), "dve")
                stok[("du", w)] = cnt["dve"]
            sec("vector", f_du)

            for t in range(NT):
                idx = w * NT + t
                spar = idx % 2

                def f_a(sc, w=w, t=t, par=par):
                    wt(sc, "sc", "sy", stok[("dsl", w)])
                    wt(sc, "sc", "dve", tok["a_neg"])
                    wt(sc, "sc", "dve", stok.get(("scans", w * NT + t - 2), 0))
                    base = cnt["act"]
                    for n in range(NST):
                        acol = sbAn[:, t * NST + n:t * NST + n + 1]
                        inc(sc.activation(
                            out=S_a[t % 2][:, n * W:(n + 1) * W],
                            in_=S_dsl[par][:, t * W:(t + 1) * W],
                            func=AF.Exp, scale=acol), "act")
                    stok[("aexp", w, t)] = base
                    if t == NT - 1:
                        stok[("aexp_done", w)] = cnt["act"]
                sec("scalar", f_a)

                def f_b(ve, w=w, t=t):
                    wt(ve, "ve", "sy", stok[("Br", w)])
                    du3 = S_du[:, t * W:(t + 1) * W].rearrange(
                        "p (o l) -> p o l", o=1).broadcast_to((128, NST, W))
                    inc(ve.tensor_tensor(
                        out=S_b[:].rearrange("p (n l) -> p n l", n=NST),
                        in0=S_Br[:].rearrange("p (n l) -> p n l", n=NST),
                        in1=du3,
                        op=ALU.mult), "dve")
                    stok[("b", w, t)] = cnt["dve"]
                    if t == NT - 1:
                        stok[("b_done", w)] = cnt["dve"]
                sec("vector", f_b)

                def f_scan(ve, w=w, t=t, idx=idx, spar=spar):
                    ab = stok[("aexp", w, t)]
                    wt(ve, "ve", "act", ab + NST)
                    sview = S_s[spar][:].rearrange("p (l n) -> p n l", n=NST)
                    last = None
                    for n in range(NST):
                        init = (0.0 if w == 0 else
                                S_carry[:, t * NST + n:t * NST + n + 1])
                        last = ve.tensor_tensor_scan(
                            out=sview[:, n, :],
                            data0=S_a[t % 2][:, n * W:(n + 1) * W],
                            data1=S_b[:, n * W:(n + 1) * W],
                            initial=init, op0=ALU.mult, op1=ALU.add)
                    v_sc = inc(last, "dve")
                    stok[("scans", idx)] = v_sc
                    wt(ve, "ve", "dve", v_sc)
                    inc(ve.tensor_copy(S_carry[:, t * NST:(t + 1) * NST],
                                       S_s[spar][:, (W - 1) * NST:W * NST]),
                        "dve")
                sec("vector", f_scan)

                def f_cm(ve, w=w, t=t, idx=idx, spar=spar):
                    wt(ve, "ve", "sy", stok[("Cr", w)])
                    inc(ve.tensor_mul(S_s[spar][:], S_s[spar][:],
                                      S_Cr[:]), "dve")
                    if t == NT - 1:
                        stok[("cm_done", w)] = cnt["dve"]
                sec("vector", f_cm)

                def f_red(ve, w=w, t=t, idx=idx, spar=spar):
                    red_src = S_s[spar]
                    wt(ve, "ve", "sy", stok.get(("ystore", w - 1), 0))
                    ve.tensor_reduce(
                        out=S_y[:, t * W:(t + 1) * W],
                        in_=red_src[:].rearrange("p (l n) -> p l n", n=NST),
                        axis=AX.X, op=ALU.add)
                    inc(ve.scalar_tensor_tensor(
                        out=S_y[:, t * W:(t + 1) * W],
                        in0=S_usl[:, t * W:(t + 1) * W],
                        scalar=sbD[:, t:t + 1],
                        in1=S_y[:, t * W:(t + 1) * W],
                        op0=ALU.mult, op1=ALU.add), "dve")
                    stok[("red", idx)] = cnt["dve"]
                    if t == NT - 1:
                        stok[("stt_done", w)] = cnt["dve"]
                sec("vector", f_red)

            def f_ys(sy, w=w):
                wt(sy, "sy", "dve", stok[("stt_done", w)])
                o = bass.AP(y_in, (w // 4) * (S * T) + (w % 4) * W,
                            [[T, 128], [128 * T, 4], [1, W]])
                inc(sy.dma_start(out=o, in_=S_y[:]), "sy", 16)
                stok[("ystore", w)] = cnt["sy"]
            sec("sync", f_ys)


        dn = {}

        def down_chunk_head(ci):
            pr, kh = divmod(ci, 4)
            lts = [2 * pr, 2 * pr + 1]

            def f_gld(sy, ci=ci, pr=pr, kh=kh, lts=lts):
                wt(sy, "sy", "gd", gu_tok[("gst_store", kh * 8 + 7)])
                wt(sy, "sy", "pe", gu_tok[("umm", 31)])
                for i, lt in enumerate(lts):
                    wt(sy, "sy", "dve", dn.get(("prod", ci - 1, i), 0))
                    ag = bass.AP(g_buf, kh * 16 * 128 * T + lt * 128,
                                 [[T, 128], [128 * T, 16], [1, 128]])
                    au = bass.AP(u_buf, kh * 16 * 128 * T + lt * 128,
                                 [[T, 128], [128 * T, 16], [1, 128]])
                    inc(sy.dma_start(out=Fg_ld[i][:], in_=ag), "sy", 16)
                    inc(sy.dma_start(out=Fu_ld[i][:], in_=au), "sy", 16)
                    dn[("gld", ci, i)] = cnt["sy"]
            sec("sync", f_gld)

            def f_prod(ve, ci=ci, pr=pr, kh=kh):
                wt(ve, "ve", "gd", gu_tok[("gst_store", 31)])
                prev = dn.get(("mmlast",) + divmod(ci - 1, 4), 0)
                wt(ve, "ve", "pe", prev)
                for i in range(2):
                    wt(ve, "ve", "sy", dn[("gld", ci, i)])
                    inc(ve.tensor_mul(F_glt[i][:], Fg_ld[i][:],
                                      Fu_ld[i][:]), "dve")
                    dn[("prod", ci, i)] = cnt["dve"]
            sec("vector", f_prod)

        def down_chunk_tail(ci):
            pr, kh = divmod(ci, 4)
            for k in range(16):
                kk = kh * 16 + k
                spar = kk % 2

                def f_sl(sc, ci=ci, pr=pr, kh=kh, k=k, kk=kk, spar=spar):
                    wt(sc, "sc", "gd", tok["repack_wdn"])
                    if ci == 0 and k < 2:
                        wt(sc, "sc", "gd", gu_tok[("gst_store", 31)])
                    wt(sc, "sc", "pe", dn.get(("kmm", pr, kh, k - 2), 0))
                    if k < 2:
                        wt(sc, "sc", "pe",
                           dn.get(("kmm", pr, kh - 1, 14 + k),
                                  dn.get(("kmm", pr - 1, 3, 14 + k), 0)))
                    inc(sc.dma_start(
                        out=F_dstr[spar][:],
                        in_=wdnTc.ap()[kk * 128:(kk + 1) * 128, :]),
                        "ad", 16)
                    dn[("str", pr, kh, k)] = cnt["ad"]
                sec("scalar", f_sl)

                def f_km(te, ci=ci, pr=pr, kh=kh, k=k, kk=kk, spar=spar):
                    for i in range(2):
                        wt(te, "te", "dve", dn[("prod", ci, i)])
                    wt(te, "te", "ad", dn[("str", pr, kh, k)])
                    if kk == 0:
                        for b in range(8):
                            bank_wait(te, "te", b)
                    m = None
                    for i in range(2):
                        lhs = F_glt[i][:, k * 128:(k + 1) * 128]
                        for c in range(4):
                            m = te.matmul(
                                P[4 * i + c][:], lhsT=lhs,
                                rhs=F_dstr[spar][:, c * 512:(c + 1) * 512],
                                start=(kk == 0), stop=(kk == 63))
                    inc(m, "pe")
                    dn[("kmm", pr, kh, k)] = cnt["pe"]
                sec("tensor", f_km)
            dn[("mmlast", pr, kh)] = cnt["pe"]

        def down_evict(pr):
            lts = [2 * pr, 2 * pr + 1]
            for i, lt in enumerate(lts):
                for c in range(4):
                    def f_ev(sc, pr=pr, i=i, c=c):
                        wt(sc, "sc", "pe", dn[("mmlast", pr, 3)])
                        jp = 4 * i + c - 2
                        wk = (("fstore", pr, jp // 4, jp % 4) if jp >= 0
                              else ("fstore", pr - 1, (jp + 8) // 4,
                                    (jp + 8) % 4))
                        wt(sc, "sc", "sy", dn.get(wk, 0))
                        v = inc(sc.activation(out=F_ffn[c % 2][:],
                                              in_=P[4 * i + c][:],
                                              func=AF.Copy), "act")
                        bank[4 * i + c] = ("act", v)
                        dn[("fev", pr, i, c)] = cnt["act"]
                    sec("scalar", f_ev)

                    def f_fs(sy, pr=pr, i=i, c=c, lt=lt):
                        wt(sy, "sy", "act", dn[("fev", pr, i, c)])
                        inc(sy.dma_start(
                            out=ffn_buf.ap()[lt * 128:(lt + 1) * 128,
                                             c * 512:(c + 1) * 512],
                            in_=F_ffn[c % 2][:]), "sy", 16)
                        dn[("fstore", pr, i, c)] = cnt["sy"]
                    sec("sync", f_fs)
            if pr == 3:
                tok["ffn_done"] = cnt["sy"]

        DS = 16
        for w in range(NW):
            ci = w - DS
            if w == 16:
                gu_ms(30)
                gu_ms(31)
            if 0 <= ci < 16:
                down_chunk_head(ci)
            scan_w(w)
            if 1 <= w <= 15:
                gu_ms(2 * (w - 1))
                gu_ms(2 * w - 1)
            if 0 <= w <= 10:
                def f_cast_late(po, w=w):
                    wt(po, "po", "cc", tok["cc_du"])
                    if w >= 1:
                        wt(po, "po", "dve", stok[("stt_done", w - 1)])
                    if w < 3:
                        ci2 = w + 1
                        rs = slice(ci2 * 2048, (ci2 + 1) * 2048)
                        inc(po.dma_start(out=wgTb.ap()[:, rs],
                                         in_=wgT[:, rs]), "gd", 16)
                        tok[("staged", "wg", ci2)] = cnt["gd"]
                        wt(po, "po", "gd", cnt["gd"])
                        inc(po.dma_start(out=wuTb.ap()[:, rs],
                                         in_=wuT[:, rs]), "gd", 16)
                        tok[("staged", "wu", ci2)] = cnt["gd"]
                        wt(po, "po", "gd", cnt["gd"])
                    elif w < 7:
                        pc = w - 3
                        rs = slice(pc * 2048, (pc + 1) * 2048)
                        inc(po.dma_start(out=wdnTc.ap()[rs, :],
                                         in_=wdnT[rs, :]), "gd", 16)
                        wt(po, "po", "gd", cnt["gd"])
                        if pc == 3:
                            tok["repack_wdn"] = cnt["gd"]
                    else:
                        pc = w - 7
                        rs = slice(pc * 1024, (pc + 1) * 1024)
                        inc(po.dma_start(out=woutTc.ap()[rs, :],
                                         in_=woutT[rs, :]), "gd", 16)
                        wt(po, "po", "gd", cnt["gd"])
                        if pc == 10 - 7:
                            tok["repack_wout"] = cnt["gd"]
                sec("gpsimd", f_cast_late)
            if 0 <= ci < 16:
                down_chunk_tail(ci)
            if w >= DS + 3 and (w - DS - 3) % 4 == 0 and (w - DS - 3) // 4 < 4:
                down_evict((w - DS - 3) // 4)

        def f_warm(te):
            for b in range(8):
                bank_wait(te, "te", b)
            m = None
            for _ in range(800):
                m = te.matmul(P[7][:], lhsT=P5_wo[0][:, 0:128],
                              rhs=P5_wo[1][:, 0:512], start=True, stop=True)
            inc(m, "pe")
            tok["warm"] = cnt["pe"]
        sec("tensor", f_warm)

        def f_agy(po):
            wt(po, "po", "sy", stok[("ystore", NW - 1)])
            wt(po, "po", "sy", tok["ffn_done"])
            wt(po, "po", "pe", dn[("mmlast", 3, 3)])
            m = po.collective_compute("AllToAll", ALU.bypass,
                                      replica_groups=RG,
                                      ins=[y_in.ap()], outs=[y_loc.ap()])
            inc(m, "cc")
            wt(po, "po", "cc", cnt["cc"])
            m = po.collective_compute("AllGather", ALU.bypass,
                                      replica_groups=RG,
                                      ins=[fence_in.ap()],
                                      outs=[fence_out[1].ap()])
            inc(m, "cc")
            tok["cc_y"] = cnt["cc"]
        sec("gpsimd", f_agy)

        p5 = {}
        for k in range(32):
            zr = k % 2

            def f_sy(sy, k=k, zr=zr):
                wt(sy, "sy", "cc", tok["cc_y"])
                wt(sy, "sy", "dve", p5.get(("y2", k - 2), 0))
                inc(sy.dma_start(out=P5_zs[zr][:],
                                 in_=z_buf.ap()[k * 128:(k + 1) * 128, :]),
                    "sy", 16)
                p5[("z", k)] = cnt["sy"]
                inc(sy.dma_start(out=P5_y[zr][:],
                                 in_=y_loc.ap()[k * 128:(k + 1) * 128, :]),
                    "sy", 16)
                p5[("y", k)] = cnt["sy"]
            sec("sync", f_sy)

            def f_sc(sc, k=k, zr=zr):
                wt(sc, "sc", "sy", p5[("z", k)])
                wt(sc, "sc", "dve", p5.get(("y2", k - 2), 0))
                inc(sc.activation(out=P5_zs[zr][:], in_=P5_zs[zr][:],
                                  func=AF.Silu), "act")
                p5[("silu", k)] = cnt["act"]
            sec("scalar", f_sc)

            def f_ve(ve, k=k, zr=zr):
                wt(ve, "ve", "act", p5[("silu", k)])
                wt(ve, "ve", "sy", p5[("y", k)])
                inc(ve.tensor_mul(P5_y2[:, k * 1024:(k + 1) * 1024],
                                  P5_y[zr][:], P5_zs[zr][:]), "dve")
                p5[("y2", k)] = cnt["dve"]
            sec("vector", f_ve)
        y2_done = cnt["dve"]

        def f_brow(sy):
            wt(sy, "sy", "cc", tok["cc_y"])
            inc(sy.dma_start(out=P5_hb[0:1, 0:2048], in_=bout), "sy", 16)
            tok["bout_ld"] = cnt["sy"]
        sec("sync", f_brow)

        def f_brep_te(te):
            wt(te, "te", "sy", tok["bout_ld"])
            for b in range(4):
                bank_wait(te, "te", b)
            m = None
            for c in range(4):
                m = te.matmul(P[c][:], lhsT=sbONES[:],
                              rhs=P5_hb[0:1, c * 512:(c + 1) * 512],
                              start=True, stop=True)
            inc(m, "pe")
            tok["brep_mm"] = cnt["pe"]
        sec("tensor", f_brep_te)

        def f_brep_sc(sc):
            wt(sc, "sc", "pe", tok["brep_mm"])
            for c in range(4):
                v = inc(sc.activation(out=P5_brep[:, c * 512:(c + 1) * 512],
                                      in_=P[c][:], func=AF.Copy), "act")
                bank[c] = ("act", v)
            tok["brep"] = cnt["act"]
        sec("scalar", f_brep_sc)

        for pr in range(4):
            lts = [2 * pr, 2 * pr + 1]
            for k in range(32):
                spar = k % 2

                def f_wl(sy, pr=pr, k=k, spar=spar):
                    wt(sy, "sy", "gd", tok["repack_wout"])
                    wt(sy, "sy", "pe", p5.get(("womm", pr, k - 2),
                                              p5.get(("womm", pr - 1, 30 + k),
                                                     0)))
                    inc(sy.dma_start(
                        out=P5_wo[spar][:],
                        in_=woutTc.ap()[k * 128:(k + 1) * 128, :]), "sy", 16)
                    p5[("wo", pr, k)] = cnt["sy"]
                sec("sync", f_wl)

                def f_wm(te, pr=pr, k=k, spar=spar, lts=lts):
                    wt(te, "te", "dve", y2_done)
                    wt(te, "te", "sy", p5[("wo", pr, k)])
                    if k == 0:
                        for b in range(8):
                            bank_wait(te, "te", b)
                    m = None
                    for i in range(2):
                        lhs = P5_y2[:, k * 1024 + lts[i] * 128:
                                    k * 1024 + (lts[i] + 1) * 128]
                        for c in range(4):
                            m = te.matmul(
                                P[4 * i + c][:], lhsT=lhs,
                                rhs=P5_wo[spar][:, c * 512:(c + 1) * 512],
                                start=(k == 0), stop=(k == 31))
                    inc(m, "pe")
                    p5[("womm", pr, k)] = cnt["pe"]
                sec("tensor", f_wm)
            for i in range(2):
                p5[("wmm", pr, i)] = cnt["pe"]

            for i, lt in enumerate(lts):
                def f_ld(sy, pr=pr, i=i, lt=lt):
                    wt(sy, "sy", "sy", tok["ffn_done"])
                    wt(sy, "sy", "pe", tok["brep_mm"])
                    if i == 0:
                        wt(sy, "sy", "dve", p5.get(("comb", pr - 1, 1), 0))
                    else:
                        wt(sy, "sy", "dve", p5[("comb", pr, 0)])
                    inc(sy.dma_start(
                        out=P5_ffn[:],
                        in_=ffn_buf.ap()[lt * 128:(lt + 1) * 128, :]),
                        "sy", 16)
                    p5[("ffl", pr, i)] = cnt["sy"]
                    inc(sy.dma_start(out=P5_hb[:],
                                     in_=h_main[lt * 128:(lt + 1) * 128, :]),
                        "sy", 16)
                    p5[("hl", pr, i)] = cnt["sy"]
                sec("sync", f_ld)

                def f_cb(ve, pr=pr, i=i):
                    orr = i % 2
                    wt(ve, "ve", "pe", p5[("wmm", pr, i)])
                    wt(ve, "ve", "sy", p5[("hl", pr, i)])
                    wt(ve, "ve", "act", tok["brep"])
                    wt(ve, "ve", "sy", p5.get(("ostore", pr - 1, i), 0))
                    ve.tensor_add(P5_hb[:], P5_hb[:], P5_brep[:])
                    for c in range(4):
                        ve.tensor_add(P5_out[orr][:, c * 512:(c + 1) * 512],
                                      P[4 * i + c][:],
                                      P5_ffn[:, c * 512:(c + 1) * 512])
                    v = inc(ve.tensor_add(P5_out[orr][:], P5_out[orr][:],
                                          P5_hb[:]), "dve")
                    for c in range(4):
                        bank[4 * i + c] = ("dve", v)
                    p5[("comb", pr, i)] = cnt["dve"]
                sec("vector", f_cb)

                def f_os(sy, pr=pr, i=i, lt=lt):
                    orr = i % 2
                    wt(sy, "sy", "dve", p5[("comb", pr, i)])
                    inc(sy.dma_start(out=out_o[lt * 128:(lt + 1) * 128, :],
                                     in_=P5_out[orr][:]), "sy", 16)
                    p5[("ostore", pr, i)] = cnt["sy"]
                sec("sync", f_os)

        def f_final(sy):
            sy.wait_ge(sems["sy"], cnt["sy"])
        sec("sync", f_final)

    return nc


_NC_CACHE = {}


def kernel(L0_attn_hidden_states, L0_attn_W_in, L0_attn_conv_w, L0_attn_conv_b,
           L0_attn_W_x, L0_attn_W_dt, L0_attn_b_dt, L0_attn_A_log, L0_attn_D,
           L0_attn_W_out, L0_attn_b_out, L0_ffn_w_gate, L0_ffn_w_up,
           L0_ffn_w_down):
    h = np.ascontiguousarray(np.asarray(L0_attn_hidden_states, np.float32)
                             .reshape(8192, 2048))
    Win = np.ascontiguousarray(np.asarray(L0_attn_W_in, np.float32))
    Wx = np.ascontiguousarray(np.asarray(L0_attn_W_x, np.float32))
    Wdt = np.ascontiguousarray(np.asarray(L0_attn_W_dt, np.float32))
    Wout = np.ascontiguousarray(np.asarray(L0_attn_W_out, np.float32))
    Wg = np.ascontiguousarray(np.asarray(L0_ffn_w_gate, np.float32))
    Wu = np.ascontiguousarray(np.asarray(L0_ffn_w_up, np.float32))
    Wd = np.ascontiguousarray(np.asarray(L0_ffn_w_down, np.float32))
    cw = np.asarray(L0_attn_conv_w, np.float32)
    cb = np.asarray(L0_attn_conv_b, np.float32)
    bdt_ = np.asarray(L0_attn_b_dt, np.float32)
    Alog = np.asarray(L0_attn_A_log, np.float32)
    Dv = np.asarray(L0_attn_D, np.float32)
    bo = np.asarray(L0_attn_b_out, np.float32)

    wx_h = np.ascontiguousarray(
        Wx.T.reshape(32, 128, 160).transpose(1, 0, 2).reshape(128, 5120))
    wdt_h = np.ascontiguousarray(Wdt.T)
    winT_h = np.ascontiguousarray(Win.T)
    wgT_h = np.ascontiguousarray(Wg.T)
    wuT_h = np.ascontiguousarray(Wu.T)
    wdnT_h = np.ascontiguousarray(Wd.T)
    woutT_h = np.ascontiguousarray(Wout.T)
    convw_h = np.ascontiguousarray(
        cw.reshape(32, 128, 4).transpose(1, 0, 2).reshape(128, 128))
    convb_h = np.ascontiguousarray(cb.reshape(32, 128).T)
    bdt_h = np.ascontiguousarray(bdt_.reshape(32, 128).T)
    ones_h = np.ones((1, 128), np.float32)
    id_h = np.eye(128, dtype=np.float32)
    bout_h = np.ascontiguousarray(bo.reshape(1, 2048))

    in_maps = []
    for c in range(NCORES):
        halo = (np.zeros((3, 2048), np.float32) if c == 0
                else np.ascontiguousarray(h[c * 1024 - 3:c * 1024]))
        asl = np.ascontiguousarray(
            Alog[c * 512:(c + 1) * 512].reshape(4, 128, 16)
            .transpose(1, 0, 2).reshape(128, 64))
        dsl = np.ascontiguousarray(Dv[c * 512:(c + 1) * 512].reshape(4, 128).T)
        in_maps.append(dict(
            h_main=np.ascontiguousarray(h[c * 1024:(c + 1) * 1024]),
            h_halo=halo,
            winT=winT_h, woutT=woutT_h, wgT=wgT_h, wuT=wuT_h,
            wdnT=wdnT_h, wx_in=wx_h, wdt_in=wdt_h,
            a_sl=asl, d_sl=dsl, convw=convw_h, convb=convb_h, bdt=bdt_h,
            bout=bout_h, ones1=ones_h, ident=id_h,
        ))

    if "nc" not in _NC_CACHE:
        _NC_CACHE["nc"] = build_program()
    nc = _NC_CACHE["nc"]
    res = run_bass_kernel_spmd(nc, in_maps, core_ids=list(range(NCORES)))
    out = np.concatenate([res.results[c]["o"] for c in range(NCORES)], axis=0)
    return out.reshape(1, 8192, 2048).astype(np.float32)


# version 32
# speedup vs baseline: 1.1227x; 1.0134x over previous
import contextlib
import os
import numpy as np

os.environ.setdefault("TRN_ALLOW_ICI_ALL_TO_ALL", "1")

import concourse.bass as bass
import concourse.mybir as mybir
from concourse.bass_utils import run_bass_kernel_spmd

F32 = mybir.dt.float32
BF16 = mybir.dt.bfloat16
AF = mybir.ActivationFunctionType
ALU = mybir.AluOpType
AX = mybir.AxisListType

NCORES = 8
T = 1024
D = 2048
DI = 4096
NST = 16
RNK = 128
HF = 8192
S = 512
NT = 4
W = 256
NW = 8192 // W
EPS = 1e-5
RG = [list(range(NCORES))]


def build_program():
    nc = bass.Bass("TRN2", target_bir_lowering=False, debug=False)

    def ein(name, shape, dt=F32):
        return nc.dram_tensor(name, shape, dt, kind="ExternalInput").ap()

    h_main = ein("h_main", [T, D])
    h_halo = ein("h_halo", [3, D])
    winT = ein("winT", [D, HF])
    woutT = ein("woutT", [DI, D])
    wgT = ein("wgT", [D, HF])
    wuT = ein("wuT", [D, HF])
    wdnT = ein("wdnT", [HF, D])
    wx_in = ein("wx_in", [128, 32 * 160])
    wdt_in = ein("wdt_in", [128, DI])
    a_sl = ein("a_sl", [128, NT * NST])
    d_sl = ein("d_sl", [128, NT])
    convw = ein("convw", [128, 32 * 4])
    convb = ein("convb", [128, 32])
    bdt = ein("bdt", [128, 32])
    bout = ein("bout", [1, D])
    ones1 = ein("ones1", [1, 128])
    ident = ein("ident", [128, 128])
    out_o = nc.dram_tensor("o", [T, D], F32, kind="ExternalOutput").ap()

    def idr(name, shape, dt=F32):
        return nc.dram_tensor(name, shape, dt)

    def idrs(name, shape, dt=F32):
        return nc.dram_tensor(name, shape, dt, addr_space="Shared")

    wxTb = idr("wxTb", [128, 32 * 160], BF16)
    wdtTb = idr("wdtTb", [128, DI], BF16)
    winTb = idr("winTb", [D, HF], BF16)
    wgTb = idr("wgTb", [D, HF], BF16)
    wuTb = idr("wuTb", [D, HF], BF16)
    woutTc = idr("woutTc", [DI, D], BF16)
    wdnTc = idr("wdnTc", [HF, D], BF16)
    z_buf = idr("z_buf", [DI, T])
    g_buf = idr("g_buf", [HF, T], BF16)
    u_buf = idr("u_buf", [HF, T], BF16)
    ffn_buf = idr("ffn_buf", [T, D])
    du_in = idr("du_in", [2 * DI, T])
    du_out = idr("du_out", [2 * DI, T])
    bc_in = idr("bc_in", [256, 128])
    bc_all = idr("bc_all", [NCORES * 256, 128])
    y_in = idr("y_in", [DI, T])
    y_loc = idr("y_loc", [DI, T])
    fence_in = idr("fence_in", [1, 8])
    fence_out = [idr(f"fence_out{i}", [NCORES, 8]) for i in range(2)]

    with contextlib.ExitStack() as ctx:
        def sbt(name, shape, dt=F32):
            return ctx.enter_context(nc.sbuf_tensor(name, shape, dt))

        sbA = sbt("sbA", [128, 64])
        sbAe = sbt("sbAe", [128, 64])
        sbAn = sbt("sbAn", [128, 64])
        sbD = sbt("sbD", [128, NT])
        sbCW = sbt("sbCW", [128, 128])
        sbCB = sbt("sbCB", [128, 32])
        sbBDT = sbt("sbBDT", [128, 32])
        sbONES = sbt("sbONES", [1, 128])
        sbID = sbt("sbID", [128, 128])
        sbSTAT = sbt("sbSTAT", [128, 16])

        AFSZ = 52400
        arena = sbt("arena", [128, AFSZ])
        arv = arena[:]

        def rf(off, size):
            assert off + size <= AFSZ, (off, size)
            return arv[:, off:off + size]

        def rbv(off, size_f32):
            return rf(off, size_f32).bitcast(BF16)

        XNT = rbv(0, 8216)

        def xnt(dk):
            return XNT[:, dk * 1027:(dk + 1) * 1027]

        WTB = [rbv(8216, 2048), rbv(10264, 2048)]
        RA = 12320

        A_xn = rf(RA + 0, 2048)
        A_h = rf(RA + 2048, 2048)
        A_scr = rf(RA + 4096, 2048)
        A_xt = [rf(RA + 6144, 1028), rf(RA + 7172, 1028)]
        A_hst = rf(RA + 8200, 512)
        A_trs = rf(RA + 8712, 32)
        A_bc = rf(RA + 8744, 1024)
        A_ct = rf(RA + 9768, 128)
        A_dl = [rf(RA + 9896, 1024), rf(RA + 10920, 1024)]
        A_sp = [rf(RA + 11944 + i * 512, 512) for i in range(3)]
        A_cacc = [rf(RA + 13480, 1024), rf(RA + 14504, 1024)]
        A_xcf = [rf(RA + 15528, 1024), rf(RA + 16552, 1024)]
        A_dtp = rbv(RA + 17576, 512)
        A_wxT = rbv(RA + 18088, 2560)
        A_wdtT = rbv(RA + 20648, 2048)
        A_xcb = rbv(RA + 22696, 16384)

        S_s = [rf(RA + 0, 4096), rf(RA + 0, 4096)]
        S_a = [rf(RA + 8192, 4096), rf(RA + 4096, 4096)]
        S_b = rf(RA + 12288, 4096)
        S_Br = rf(RA + 16384, 4096)
        S_Cr = rf(RA + 20480, 4096)
        S_dsl = [rf(RA + 24576, 1024), rf(RA + 25600, 1024)]
        S_usl = rf(RA + 26624, 1024)
        S_du = rf(RA + 27648, 1024)
        S_y = rf(RA + 28672, 1024)
        S_carry = rf(RA + 29696, 64)
        RFN = RA + 29760

        F_z = rf(RFN + 0, 1024)
        Fg = [[rbv(RFN + 4096 + w2 * 1024 + q * 256, 256) for q in range(4)]
              for w2 in range(2)]
        Fu = [[rbv(RFN + 6144 + w2 * 1024 + q * 256, 256) for q in range(4)]
              for w2 in range(2)]
        Fg_ld = [rbv(RFN + 0, 1024), rbv(RFN + 1024, 1024)]
        WTBg2 = [rbv(RFN + 0, 2048), rbv(RFN + 2048, 2048)]
        Fu_ld = [rbv(RFN + 2048, 1024), rbv(RFN + 3072, 1024)]
        F_glt = [rbv(RFN + 4096, 1024), rbv(RFN + 5120, 1024)]
        F_dstr = [rbv(RFN + 6144, 1024), rbv(RFN + 7168, 1024)]
        F_ffn = [rf(RFN + 8192 + i * 512, 512) for i in range(2)]

        P5_zs = [rf(RA + 0, 1024), rf(RA + 1024, 1024)]
        P5_y = [rf(RA + 2048, 1024), rf(RA + 3072, 1024)]
        P5_hb = rf(RA + 4096, 2048)
        P5_ffn = rf(RA + 6144, 2048)
        P5_brep = rf(RA + 8192, 2048)
        P5_out = [rf(RA + 10240, 2048), rf(RA + 12288, 2048)]
        P5_y2 = rbv(RA + 14336, 16384)
        P5_wo = [rbv(RA + 30720, 1024), rbv(RA + 31744, 1024)]

        P = [ctx.enter_context(nc.psum_tensor(f"ps{i}", [128, 512], F32))
             for i in range(8)]

        sems, cnt, waited = {}, {}, {}
        for s in ["sy", "gd", "ad", "cc", "pe", "act", "dve", "po"]:
            sems[s] = ctx.enter_context(nc.semaphore(s))
            cnt[s] = 0

        def inc(instr, s, k=1):
            instr.then_inc(sems[s], k)
            cnt[s] += k
            return cnt[s]

        def wt(eng, en, s, v):
            if not v or v <= 0:
                return
            wd = waited.setdefault(en, {})
            if wd.get(s, 0) >= v:
                return
            eng.wait_ge(sems[s], v)
            wd[s] = v

        block = ctx.enter_context(nc.Block())

        def sec(en, fn):
            getattr(block, en)(fn)

        tok = {}
        bank = [None] * 8

        def bank_wait(eng, en, i):
            if bank[i] is not None:
                wt(eng, en, bank[i][0], bank[i][1])

        def ph0_sy(sy):
            for dst, src in ((sbA, a_sl), (sbD, d_sl), (sbCW, convw),
                             (sbCB, convb), (sbBDT, bdt),
                             (sbONES, ones1), (sbID, ident)):
                inc(sy.dma_start(out=dst[:], in_=src), "sy", 16)
            tok["params"] = cnt["sy"]
        sec("sync", ph0_sy)

        def ph0_po(po):
            inc(po.dma_start(out=winTb.ap()[:, 0:DI], in_=winT[:, 0:DI]),
                "gd", 16)
            tok[("staged", "winx")] = cnt["gd"]
            wt(po, "po", "gd", cnt["gd"])
            for name, dst, src_ in (("wx", wxTb.ap(), wx_in),
                                    ("wdt", wdtTb.ap(), wdt_in),
                                    ("winz", winTb.ap()[:, DI:HF],
                                     winT[:, DI:HF])):
                inc(po.dma_start(out=dst, in_=src_), "gd", 16)
                tok[("staged", name)] = cnt["gd"]
                wt(po, "po", "gd", cnt["gd"])
            c0, c1 = 0, 2048
            inc(po.dma_start(out=wgTb.ap()[:, c0:c1],
                             in_=wgT[:, c0:c1]), "gd", 16)
            tok[("staged", "wg", 0)] = cnt["gd"]
            wt(po, "po", "gd", cnt["gd"])
            inc(po.dma_start(out=wuTb.ap()[:, c0:c1],
                             in_=wuT[:, c0:c1]), "gd", 16)
            tok[("staged", "wu", 0)] = cnt["gd"]
            wt(po, "po", "gd", cnt["gd"])
        sec("gpsimd", ph0_po)

        def ph0_sc_a(sc):
            wt(sc, "sc", "sy", tok["params"])
            inc(sc.activation(out=sbAe[:], in_=sbA[:], func=AF.Exp), "act")
            tok["a_exp"] = cnt["act"]
        sec("scalar", ph0_sc_a)

        def ph0_ve_a(ve):
            wt(ve, "ve", "act", tok["a_exp"])
            inc(ve.tensor_scalar_mul(sbAn[:], sbAe[:], -1.0), "dve")
            tok["a_neg"] = cnt["dve"]
        sec("vector", ph0_ve_a)

        for i in range(9):
            lt = i - 1
            if i == 0:
                def f_ms(ve):
                    inc(ve.memset(A_h[:], 0.0), "dve")
                    tok["memset_h"] = cnt["dve"]
                sec("vector", f_ms)

            def f_sy(sy, i=i, lt=lt):
                if i == 0:
                    wt(sy, "sy", "dve", tok["memset_h"])
                    inc(sy.dma_start(out=A_h[0:3, :], in_=h_halo), "sy", 16)
                else:
                    wt(sy, "sy", "dve", tok.get(("xn", i - 1), 0))
                    inc(sy.dma_start(out=A_h[:],
                                     in_=h_main[lt * 128:(lt + 1) * 128, :]),
                        "sy", 16)
                tok[("hload", i)] = cnt["sy"]
            sec("sync", f_sy)

            def f_sq(sc, i=i):
                wt(sc, "sc", "sy", tok[("hload", i)])
                wt(sc, "sc", "dve", tok.get(("lnstat", i - 1), 0))
                inc(sc.activation(out=A_scr[:], in_=A_h[:], func=AF.Square),
                    "act")
                tok[("hsq", i)] = cnt["act"]
            sec("scalar", f_sq)

            def f_ve(ve, i=i):
                wt(ve, "ve", "sy", tok[("hload", i)])
                wt(ve, "ve", "act", tok.get(("rstd", i - 1), 0))
                ve.tensor_reduce(out=sbSTAT[:, 0:1], in_=A_h[:], axis=AX.X,
                                 op=ALU.add)
                wt(ve, "ve", "act", tok[("hsq", i)])
                v0 = inc(ve.tensor_reduce(out=sbSTAT[:, 1:2], in_=A_scr[:],
                                          axis=AX.X, op=ALU.add), "dve")
                wt(ve, "ve", "dve", v0)
                v1 = inc(ve.tensor_scalar_mul(sbSTAT[:, 3:4], sbSTAT[:, 0:1],
                                              -1.0 / 2048.0), "dve")
                wt(ve, "ve", "dve", v1)
                v2 = inc(ve.tensor_mul(sbSTAT[:, 4:5], sbSTAT[:, 3:4],
                                       sbSTAT[:, 3:4]), "dve")
                v3 = inc(ve.tensor_scalar_mul(sbSTAT[:, 5:6], sbSTAT[:, 1:2],
                                              1.0 / 2048.0), "dve")
                wt(ve, "ve", "dve", v3)
                v4 = inc(ve.tensor_sub(sbSTAT[:, 6:7], sbSTAT[:, 5:6],
                                       sbSTAT[:, 4:5]), "dve")
                wt(ve, "ve", "dve", v4)
                inc(ve.tensor_scalar_add(sbSTAT[:, 6:7], sbSTAT[:, 6:7],
                                         EPS), "dve")
                tok[("lnstat", i)] = cnt["dve"]
            sec("vector", f_ve)

            def f_sc(sc, i=i):
                wt(sc, "sc", "dve", tok[("lnstat", i)])
                a0 = inc(sc.activation(out=sbSTAT[:, 8:9],
                                       in_=sbSTAT[:, 6:7], func=AF.Ln),
                         "act")
                wt(sc, "sc", "act", a0)
                inc(sc.activation(out=sbSTAT[:, 7:8], in_=sbSTAT[:, 8:9],
                                  func=AF.Exp, scale=-0.5), "act")
                tok[("rstd", i)] = cnt["act"]
            sec("scalar", f_sc)

            def f_ve2(ve, i=i):
                wt(ve, "ve", "act", tok[("rstd", i)])
                wt(ve, "ve", "pe", tok.get(("tp", i - 1, 15), 0))
                inc(ve.tensor_scalar(out=A_xn[:], in0=A_h[:],
                                     scalar1=sbSTAT[:, 3:4],
                                     scalar2=sbSTAT[:, 7:8],
                                     op0=ALU.add, op1=ALU.mult), "dve")
                tok[("xn", i)] = cnt["dve"]
            sec("vector", f_ve2)

            for dk in range(16):
                b = 4 + dk % 2

                def f_te(te, i=i, dk=dk, b=b):
                    wt(te, "te", "sy", tok["params"])
                    wt(te, "te", "dve", tok[("xn", i)])
                    bank_wait(te, "te", b)
                    m = te.transpose(P[b][:, 0:128],
                                     in_=A_xn[:, dk * 128:(dk + 1) * 128],
                                     identity=sbID[:])
                    inc(m, "pe")
                    tok[("tp", i, dk)] = cnt["pe"]
                sec("tensor", f_te)

                def f_sc2(sc, i=i, dk=dk, b=b, lt=lt):
                    wt(sc, "sc", "pe", tok[("tp", i, dk)])
                    if i == 0:
                        o, src = xnt(dk)[:, 0:3], P[b][:, 0:3]
                    else:
                        o = xnt(dk)[:, 3 + lt * 128:3 + (lt + 1) * 128]
                        src = P[b][:, 0:128]
                    v = inc(sc.activation(out=o, in_=src, func=AF.Copy), "act")
                    bank[b] = ("act", v)
                sec("scalar", f_sc2)
        XNT_done = cnt["act"]

        def f_wx(sy):
            wt(sy, "sy", "gd", tok[("staged", "wx")])
            inc(sy.dma_start(out=A_wxT[:, 0:5120], in_=wxTb.ap()), "sy", 16)
            tok["wxT"] = cnt["sy"]
            wt(sy, "sy", "gd", tok[("staged", "wdt")])
            inc(sy.dma_start(out=A_wdtT[:, 0:4096], in_=wdtTb.ap()),
                "sy", 16)
            tok["wdtT"] = cnt["sy"]
        sec("sync", f_wx)



        wt_lastmm = [0, 0]

        def win_block_ap(src_all, e0):
            return bass.AP(src_all, e0,
                           [[HF, 128], [128 * HF, 16], [1, 256]])

        for ms in range(16):
            par = ms % 2
            mg = [2 * ms, 2 * ms + 1]

            def f_sy(sy, ms=ms, par=par):
                wt(sy, "sy", "gd", tok[("staged", "winx")])
                wt(sy, "sy", "pe", wt_lastmm[par])
                inc(sy.dma_start(out=WTB[par][:, 0:4096],
                                 in_=win_block_ap(winTb, ms * 256)), "sy", 16)
                tok[("wtx", ms)] = cnt["sy"]
            sec("sync", f_sy)

            def f_te(te, ms=ms, par=par):
                wt(te, "te", "sy", tok[("wtx", ms)])
                wt(te, "te", "act", XNT_done)
                for b in range(4):
                    bank_wait(te, "te", b)
                bank_wait(te, "te", 4)
                m = None
                for k in range(16):
                    for j in range(2):
                        lhs = WTB[par][:, k * 256 + j * 128:
                                       k * 256 + (j + 1) * 128]
                        te.matmul(P[2 * j][:], lhsT=lhs,
                                  rhs=xnt(k)[:, 3:515],
                                  start=(k == 0), stop=(k == 15))
                        m = te.matmul(P[2 * j + 1][:], lhsT=lhs,
                                      rhs=xnt(k)[:, 515:1027],
                                      start=(k == 0), stop=(k == 15))
                inc(m, "pe")
                tok[("xmm", ms)] = cnt["pe"]
                for k in range(16):
                    m = te.matmul(P[4][0:3, 0:256], lhsT=xnt(k)[:, 0:3],
                                  rhs=WTB[par][:, k * 256:(k + 1) * 256],
                                  start=(k == 0), stop=(k == 15))
                inc(m, "pe")
                tok[("hmm", ms)] = cnt["pe"]
                wt_lastmm[par] = cnt["pe"]
            sec("tensor", f_te)

            def f_sc(sc, ms=ms):
                wt(sc, "sc", "pe", tok[("xmm", ms)])
                for j in range(2):
                    wt(sc, "sc", "dve", tok.get(("conv", ms - 1, j), 0))
                    v = inc(sc.activation(out=A_xt[j][:, 3:515],
                                          in_=P[2 * j][:], func=AF.Copy),
                            "act")
                    bank[2 * j] = ("act", v)
                    v = inc(sc.activation(out=A_xt[j][:, 515:1027],
                                          in_=P[2 * j + 1][:], func=AF.Copy),
                            "act")
                    bank[2 * j + 1] = ("act", v)
                wt(sc, "sc", "pe", tok[("hmm", ms)])
                v = inc(sc.activation(out=A_hst[0:3, 0:256],
                                      in_=P[4][0:3, 0:256], func=AF.Copy),
                        "act")
                bank[4] = ("act", v)
                tok[("xev", ms)] = cnt["act"]
            sec("scalar", f_sc)

            def f_ve(ve, ms=ms, mg=mg):
                wt(ve, "ve", "act", tok[("xev", ms)])
                for blk in range(8):
                    ve.transpose(A_scr[0:32, blk * 32:(blk + 1) * 32],
                                 A_hst[0:32, blk * 32:(blk + 1) * 32])
                for blk in range(8):
                    j = blk // 4
                    dbase = (blk % 4) * 32
                    ve.tensor_copy(A_xt[j][dbase:dbase + 32, 0:3],
                                   A_scr[0:32, blk * 32 :blk * 32 + 3])
                for j in range(2):
                    m = mg[j]
                    ve.tensor_scalar_mul(A_cacc[j][:], A_xt[j][:, 0:1024],
                                         sbCW[:, 4 * m:4 * m + 1])
                    last = None
                    for kk in range(1, 4):
                        last = ve.scalar_tensor_tensor(
                            out=A_cacc[j][:], in0=A_xt[j][:, kk:kk + 1024],
                            scalar=sbCW[:, 4 * m + kk:4 * m + kk + 1],
                            in1=A_cacc[j][:], op0=ALU.mult, op1=ALU.add)
                    inc(last, "dve")
                    tok[("conv", ms, j)] = cnt["dve"]
            sec("vector", f_ve)

            def f_sc2(sc, ms=ms, mg=mg):
                for j in range(2):
                    m = mg[j]
                    wt(sc, "sc", "dve", tok[("conv", ms, j)])
                    wt(sc, "sc", "sy", tok.get(("ustore", ms - 1, j), 0))
                    wt(sc, "sc", "dve", tok.get(("xcb", ms - 1, j), 0))
                    inc(sc.activation(out=A_xcf[j][:], in_=A_cacc[j][:],
                                      func=AF.Silu, bias=sbCB[:, m:m + 1]),
                        "act")
                    tok[("xcf", ms, j)] = cnt["act"]
            sec("scalar", f_sc2)

            def f_sy2(sy, ms=ms, mg=mg):
                for j in range(2):
                    m = mg[j]
                    wt(sy, "sy", "act", tok[("xcf", ms, j)])
                    r0 = (m // 4) * 1024 + 512 + (m % 4) * 128
                    inc(sy.dma_start(
                        out=du_in.ap()[r0:r0 + 128, :],
                        in_=A_xcf[j][:]), "sy", 16)
                    tok[("ustore", ms, j)] = cnt["sy"]
            sec("sync", f_sy2)

            def f_ve2(ve, ms=ms, mg=mg):
                for j in range(2):
                    m = mg[j]
                    wt(ve, "ve", "act", tok[("xcf", ms, j)])
                    inc(ve.tensor_copy(A_xcb[:, m * 1024:(m + 1) * 1024],
                                       A_xcf[j][:]), "dve")
                    tok[("xcb", ms, j)] = cnt["dve"]
            sec("vector", f_ve2)
        xcb_done = cnt["dve"]

        def f_wx_te(te):
            wt(te, "te", "sy", tok["wxT"])
            wt(te, "te", "dve", xcb_done)
            for b in range(4):
                bank_wait(te, "te", b)
            m = None
            for k in range(32):
                lhs0 = A_wxT[:, k * 160:k * 160 + 128]
                lhs1 = A_wxT[:, k * 160 + 128:(k + 1) * 160]
                mv0 = A_xcb[:, k * 1024:k * 1024 + 512]
                mv1 = A_xcb[:, k * 1024 + 512:(k + 1) * 1024]
                te.matmul(P[0][:], lhsT=lhs0, rhs=mv0, start=(k == 0),
                          stop=(k == 31))
                te.matmul(P[1][:], lhsT=lhs0, rhs=mv1, start=(k == 0),
                          stop=(k == 31))
                te.matmul(P[2][0:32, :], lhsT=lhs1, rhs=mv0, start=(k == 0),
                          stop=(k == 31))
                m = te.matmul(P[3][0:32, :], lhsT=lhs1, rhs=mv1,
                              start=(k == 0), stop=(k == 31))
            inc(m, "pe")
            tok["wx_mm"] = cnt["pe"]
        sec("tensor", f_wx_te)

        def f_wx_sc(sc):
            wt(sc, "sc", "pe", tok["wx_mm"])
            v = inc(sc.activation(out=A_dtp[:, 0:512], in_=P[0][:],
                                  func=AF.Copy), "act")
            bank[0] = ("act", v)
            v = inc(sc.activation(out=A_dtp[:, 512:1024], in_=P[1][:],
                                  func=AF.Copy), "act")
            bank[1] = ("act", v)
            v = inc(sc.activation(out=A_bc[0:32, 0:512], in_=P[2][0:32, :],
                                  func=AF.Copy), "act")
            bank[2] = ("act", v)
            v = inc(sc.activation(out=A_bc[0:32, 512:1024],
                                  in_=P[3][0:32, :], func=AF.Copy), "act")
            bank[3] = ("act", v)
            tok["bc_ev"] = cnt["act"]
        sec("scalar", f_wx_sc)

        def f_b_store(sy):
            wt(sy, "sy", "act", tok["bc_ev"])
            o = bass.AP(bc_in, 0, [[1024, 16], [1, 1024]])
            inc(sy.dma_start(out=o, in_=A_bc[0:16, :]), "sy", 16)
        sec("sync", f_b_store)

        for c8 in range(8):
            def f_t(te, c8=c8):
                wt(te, "te", "act", tok["bc_ev"])
                bank_wait(te, "te", 4)
                m = te.transpose(P[4][:, 0:32],
                                 in_=A_bc[0:32, c8 * 128:(c8 + 1) * 128],
                                 identity=sbID[0:32, 0:32])
                inc(m, "pe")
                tok[("ctp", c8)] = cnt["pe"]
            sec("tensor", f_t)

            def f_s(sc, c8=c8):
                wt(sc, "sc", "pe", tok[("ctp", c8)])
                v = inc(sc.activation(out=A_ct[:, c8 * 16:(c8 + 1) * 16],
                                      in_=P[4][:, 16:32], func=AF.Copy),
                        "act")
                bank[4] = ("act", v)
            sec("scalar", f_s)
        tok["ct_ev"] = cnt["act"]

        def f_c_store(sy):
            wt(sy, "sy", "act", tok["ct_ev"])
            o = bass.AP(bc_in, 16384, [[16, 128], [2048, 8], [1, 16]])
            inc(sy.dma_start(out=o, in_=A_ct[:]), "sy", 16)
        sec("sync", f_c_store)

        last_dl = [0]
        for m in range(32):
            mr = m % 2

            def f_t(te, m=m):
                wt(te, "te", "sy", tok["wdtT"])
                wt(te, "te", "act", tok["bc_ev"])
                bank_wait(te, "te", 5)
                bank_wait(te, "te", 6)
                lhs = A_wdtT[:, m * 128:(m + 1) * 128]
                te.matmul(P[5][:], lhsT=lhs, rhs=A_dtp[:, 0:512],
                          start=True, stop=True)
                m2 = te.matmul(P[6][:], lhsT=lhs, rhs=A_dtp[:, 512:1024],
                               start=True, stop=True)
                inc(m2, "pe")
                tok[("dtmm", m)] = cnt["pe"]
            sec("tensor", f_t)

            def f_s(sc, m=m):
                wt(sc, "sc", "pe", tok[("dtmm", m)])
                wt(sc, "sc", "dve", last_dl[0])
                bcol = sbBDT[:, m:m + 1]
                SP0 = [A_sp[0][:], A_cacc[0][:, 0:512]]
                SP1 = [A_sp[1][:], A_cacc[0][:, 512:1024]]
                SP2 = [A_sp[2][:], A_cacc[1][:, 0:512]]
                for ch in range(2):
                    sc.activation(out=SP0[ch], in_=P[5 + ch][:],
                                  func=AF.Relu, bias=bcol)
                for ch in range(2):
                    sc.activation(out=SP1[ch], in_=P[5 + ch][:],
                                  func=AF.Abs, bias=bcol)
                for ch in range(2):
                    sc.activation(out=SP2[ch], in_=SP1[ch],
                                  func=AF.Exp, scale=-1.0)
                v = None
                for ch in range(2):
                    v = inc(sc.activation(out=SP1[ch], in_=SP2[ch],
                                          func=AF.Ln, bias=1.0), "act")
                bank[5] = ("act", v)
                bank[6] = ("act", v)
                tok[("spl", m)] = cnt["act"]
            sec("scalar", f_s)

            def f_v(ve, m=m, mr=mr):
                SP0 = [A_sp[0][:], A_cacc[0][:, 0:512]]
                SP1 = [A_sp[1][:], A_cacc[0][:, 512:1024]]
                wt(ve, "ve", "act", tok[("spl", m)])
                wt(ve, "ve", "sy", tok.get(("dstore", m - 2), 0))
                for ch in range(2):
                    inc(ve.tensor_add(A_dl[mr][:, ch * 512:(ch + 1) * 512],
                                      SP1[ch], SP0[ch]), "dve")
                tok[("dl", m, 1)] = cnt["dve"]
                last_dl[0] = cnt["dve"]
            sec("vector", f_v)

            def f_y(sy, m=m, mr=mr):
                wt(sy, "sy", "dve", tok[("dl", m, 1)])
                r0 = (m // 4) * 1024 + (m % 4) * 128
                inc(sy.dma_start(out=du_in.ap()[r0:r0 + 128, :],
                                 in_=A_dl[mr][:]), "sy", 16)
                tok[("dstore", m)] = cnt["sy"]
            sec("sync", f_y)
        tok["du_done"] = cnt["sy"]

        def f_agdu(po):
            wt(po, "po", "sy", tok["du_done"])
            wt(po, "po", "gd", tok[("staged", "wu", 0)])
            m = po.collective_compute("AllToAll", ALU.bypass,
                                      replica_groups=RG,
                                      ins=[du_in.ap()], outs=[du_out.ap()])
            inc(m, "cc")
            m = po.collective_compute("AllGather", ALU.bypass,
                                      replica_groups=RG,
                                      ins=[bc_in.ap()], outs=[bc_all.ap()])
            inc(m, "cc")
            wt(po, "po", "cc", cnt["cc"])
            m = po.collective_compute("AllGather", ALU.bypass,
                                      replica_groups=RG,
                                      ins=[fence_in.ap()],
                                      outs=[fence_out[0].ap()])
            inc(m, "cc")
            tok["cc_du"] = cnt["cc"]
            tok["cc_bc"] = cnt["cc"]
        sec("gpsimd", f_agdu)

        for ms in range(16):
            par = ms % 2

            def f_sy(sy, ms=ms, par=par):
                wt(sy, "sy", "gd", tok[("staged", "winz")])
                wt(sy, "sy", "pe", wt_lastmm[par])
                inc(sy.dma_start(out=WTB[par][:, 0:4096],
                                 in_=win_block_ap(winTb, 4096 + ms * 256)),
                    "sy", 16)
                tok[("wtz", ms)] = cnt["sy"]
            sec("sync", f_sy)

            def f_te(te, ms=ms, par=par):
                wt(te, "te", "sy", tok[("wtz", ms)])
                for b in range(4):
                    bank_wait(te, "te", b)
                m = None
                for k in range(16):
                    for j in range(2):
                        lhs = WTB[par][:, k * 256 + j * 128:
                                       k * 256 + (j + 1) * 128]
                        te.matmul(P[2 * j][:], lhsT=lhs,
                                  rhs=xnt(k)[:, 3:515],
                                  start=(k == 0), stop=(k == 15))
                        m = te.matmul(P[2 * j + 1][:], lhsT=lhs,
                                      rhs=xnt(k)[:, 515:1027],
                                      start=(k == 0), stop=(k == 15))
                inc(m, "pe")
                tok[("zmm", ms)] = cnt["pe"]
                wt_lastmm[par] = cnt["pe"]
            sec("tensor", f_te)

            for j in range(2):
                def f_sc(sc, ms=ms, j=j):
                    wt(sc, "sc", "pe", tok[("zmm", ms)])
                    wt(sc, "sc", "sy", tok.get(("zstore", ms, j - 1),
                                               tok.get(("zstore", ms - 1, 1),
                                                       0)))
                    v = inc(sc.activation(out=F_z[:, 0:512],
                                          in_=P[2 * j][:], func=AF.Copy),
                            "act")
                    bank[2 * j] = ("act", v)
                    v = inc(sc.activation(out=F_z[:, 512:1024],
                                          in_=P[2 * j + 1][:], func=AF.Copy),
                            "act")
                    bank[2 * j + 1] = ("act", v)
                    tok[("zev", ms, j)] = cnt["act"]
                sec("scalar", f_sc)

                def f_sy2(sy, ms=ms, j=j):
                    wt(sy, "sy", "act", tok[("zev", ms, j)])
                    mz = 2 * ms + j
                    inc(sy.dma_start(
                        out=z_buf.ap()[mz * 128:(mz + 1) * 128, :],
                        in_=F_z[:]), "sy", 16)
                    tok[("zstore", ms, j)] = cnt["sy"]
                sec("sync", f_sy2)

        zlast = (wt_lastmm[0], wt_lastmm[1])

        stok = {}
        gu_tok = {}

        def gu_ms(ms):
            gt = [WTB[0], WTB[1]] if ms % 2 == 0 else [WTBg2[0], WTBg2[1]]

            def f_sy(sy, ms=ms, gt=gt):
                wt(sy, "sy", "gd", tok[("staged", "wg", ms // 8)])
                wt(sy, "sy", "pe", gu_tok.get(("gmm", ms - 2), zlast[0]))
                inc(sy.dma_start(out=gt[0][:, 0:4096],
                                 in_=win_block_ap(wgTb, ms * 256)), "sy", 16)
                gu_tok[("wg", ms)] = cnt["sy"]
                wt(sy, "sy", "gd", tok[("staged", "wu", ms // 8)])
                wt(sy, "sy", "pe", gu_tok.get(("umm", ms - 2), zlast[1]))
                inc(sy.dma_start(out=gt[1][:, 0:4096],
                                 in_=win_block_ap(wuTb, ms * 256)), "sy", 16)
                gu_tok[("wu", ms)] = cnt["sy"]
            sec("sync", f_sy)

            def f_te(te, ms=ms, gt=gt):
                wt(te, "te", "sy", gu_tok[("wg", ms)])
                for b in range(8):
                    bank_wait(te, "te", b)
                m = None
                for k in range(16):
                    for j in range(2):
                        lhs = gt[0][:, k * 256 + j * 128:
                                    k * 256 + (j + 1) * 128]
                        te.matmul(P[2 * j][:], lhsT=lhs,
                                  rhs=xnt(k)[:, 3:515],
                                  start=(k == 0), stop=(k == 15))
                        m = te.matmul(P[2 * j + 1][:], lhsT=lhs,
                                      rhs=xnt(k)[:, 515:1027],
                                      start=(k == 0), stop=(k == 15))
                inc(m, "pe")
                gu_tok[("gmm", ms)] = cnt["pe"]
                wt(te, "te", "sy", gu_tok[("wu", ms)])
                for k in range(16):
                    for j in range(2):
                        lhs = gt[1][:, k * 256 + j * 128:
                                    k * 256 + (j + 1) * 128]
                        te.matmul(P[4 + 2 * j][:], lhsT=lhs,
                                  rhs=xnt(k)[:, 3:515],
                                  start=(k == 0), stop=(k == 15))
                        m = te.matmul(P[5 + 2 * j][:], lhsT=lhs,
                                      rhs=xnt(k)[:, 515:1027],
                                      start=(k == 0), stop=(k == 15))
                inc(m, "pe")
                gu_tok[("umm", ms)] = cnt["pe"]
            sec("tensor", f_te)

            def f_sc(sc, ms=ms):
                w2 = ms % 2
                wt(sc, "sc", "pe", gu_tok[("gmm", ms)])
                wt(sc, "sc", "gd", gu_tok.get(("gst_store", ms - 2), 0))
                for q in range(4):
                    v = inc(sc.activation(out=Fg[w2][q][:], in_=P[q][:],
                                          func=AF.Silu), "act")
                    bank[q] = ("act", v)
                wt(sc, "sc", "pe", gu_tok[("umm", ms)])
                for q in range(4):
                    v = inc(sc.activation(out=Fu[w2][q][:], in_=P[4 + q][:],
                                          func=AF.Copy), "act")
                    bank[4 + q] = ("act", v)
                gu_tok[("guev", ms)] = cnt["act"]
            sec("scalar", f_sc)

            def f_sy2(po, ms=ms):
                w2 = ms % 2
                wt(po, "po", "act", gu_tok[("guev", ms)])
                for j in range(2):
                    r0 = (2 * ms + j) * 128
                    for jj in range(2):
                        q = 2 * j + jj
                        inc(po.dma_start(
                            out=g_buf.ap()[r0:r0 + 128,
                                           jj * 512:(jj + 1) * 512],
                            in_=Fg[w2][q][:]), "gd", 16)
                        inc(po.dma_start(
                            out=u_buf.ap()[r0:r0 + 128,
                                           jj * 512:(jj + 1) * 512],
                            in_=Fu[w2][q][:]), "gd", 16)
                wt(po, "po", "gd", cnt["gd"])
                gu_tok[("gst_store", ms)] = cnt["gd"]
            sec("gpsimd", f_sy2)

        def scan_w(w):
            par = w % 2

            def f_ld(sy, w=w, par=par):
                rb_i = w // 4
                l0 = w * W - rb_i * 1024
                wt(sy, "sy", "cc", tok["cc_du"])
                wt(sy, "sy", "act", stok.get(("aexp_done", w - 2), 0))
                duo = rb_i * (1024 * T) + l0
                inc(sy.dma_start(
                    out=S_dsl[par][:],
                    in_=bass.AP(du_out, duo,
                                [[T, 128], [128 * T, 4], [1, W]])),
                    "sy", 16)
                stok[("dsl", w)] = cnt["sy"]
                wt(sy, "sy", "dve", stok.get(("stt_done", w - 1), 0))
                inc(sy.dma_start(
                    out=S_usl[:],
                    in_=bass.AP(du_out, duo + 512 * T,
                                [[T, 128], [128 * T, 4], [1, W]])),
                    "sy", 16)
                stok[("usl", w)] = cnt["sy"]
                wt(sy, "sy", "cc", tok["cc_bc"])
                wt(sy, "sy", "dve", stok.get(("b_done", w - 1), 0))
                offb = rb_i * 256 * 128
                inc(sy.dma_start(
                    out=S_Br[:],
                    in_=bass.AP(bc_all, offb + l0,
                                [[0, 128], [1024, 16], [1, W]])), "sy", 16)
                stok[("Br", w)] = cnt["sy"]
                wt(sy, "sy", "dve", stok.get(("cm_done", w - 1), 0))
                inc(sy.dma_start(
                    out=S_Cr[:],
                    in_=bass.AP(bc_all, offb + 16384 + l0 * 16,
                                [[0, 128], [16, W], [1, 16]])), "sy", 16)
                stok[("Cr", w)] = cnt["sy"]
            sec("sync", f_ld)

            def f_du(ve, w=w, par=par):
                wt(ve, "ve", "sy", stok[("usl", w)])
                inc(ve.tensor_mul(S_du[:], S_dsl[par][:], S_usl[:]), "dve")
                stok[("du", w)] = cnt["dve"]
            sec("vector", f_du)

            for t in range(NT):
                idx = w * NT + t
                spar = idx % 2

                def f_a(sc, w=w, t=t, par=par):
                    wt(sc, "sc", "sy", stok[("dsl", w)])
                    wt(sc, "sc", "dve", tok["a_neg"])
                    wt(sc, "sc", "dve", stok.get(("scans", w * NT + t - 2), 0))
                    base = cnt["act"]
                    for n in range(NST):
                        acol = sbAn[:, t * NST + n:t * NST + n + 1]
                        inc(sc.activation(
                            out=S_a[t % 2][:, n * W:(n + 1) * W],
                            in_=S_dsl[par][:, t * W:(t + 1) * W],
                            func=AF.Exp, scale=acol), "act")
                    stok[("aexp", w, t)] = base
                    if t == NT - 1:
                        stok[("aexp_done", w)] = cnt["act"]
                sec("scalar", f_a)

                def f_b(ve, w=w, t=t):
                    wt(ve, "ve", "sy", stok[("Br", w)])
                    du3 = S_du[:, t * W:(t + 1) * W].rearrange(
                        "p (o l) -> p o l", o=1).broadcast_to((128, NST, W))
                    inc(ve.tensor_tensor(
                        out=S_b[:].rearrange("p (n l) -> p n l", n=NST),
                        in0=S_Br[:].rearrange("p (n l) -> p n l", n=NST),
                        in1=du3,
                        op=ALU.mult), "dve")
                    stok[("b", w, t)] = cnt["dve"]
                    if t == NT - 1:
                        stok[("b_done", w)] = cnt["dve"]
                sec("vector", f_b)

                def f_scan(ve, w=w, t=t, idx=idx, spar=spar):
                    ab = stok[("aexp", w, t)]
                    wt(ve, "ve", "act", ab + NST)
                    sview = S_s[spar][:].rearrange("p (l n) -> p n l", n=NST)
                    last = None
                    for n in range(NST):
                        init = (0.0 if w == 0 else
                                S_carry[:, t * NST + n:t * NST + n + 1])
                        last = ve.tensor_tensor_scan(
                            out=sview[:, n, :],
                            data0=S_a[t % 2][:, n * W:(n + 1) * W],
                            data1=S_b[:, n * W:(n + 1) * W],
                            initial=init, op0=ALU.mult, op1=ALU.add)
                    v_sc = inc(last, "dve")
                    stok[("scans", idx)] = v_sc
                    wt(ve, "ve", "dve", v_sc)
                    inc(ve.tensor_copy(S_carry[:, t * NST:(t + 1) * NST],
                                       S_s[spar][:, (W - 1) * NST:W * NST]),
                        "dve")
                sec("vector", f_scan)

                def f_cm(ve, w=w, t=t, idx=idx, spar=spar):
                    wt(ve, "ve", "sy", stok[("Cr", w)])
                    inc(ve.tensor_mul(S_s[spar][:], S_s[spar][:],
                                      S_Cr[:]), "dve")
                    if t == NT - 1:
                        stok[("cm_done", w)] = cnt["dve"]
                sec("vector", f_cm)

                def f_red(ve, w=w, t=t, idx=idx, spar=spar):
                    red_src = S_s[spar]
                    wt(ve, "ve", "sy", stok.get(("ystore", w - 1), 0))
                    ve.tensor_reduce(
                        out=S_y[:, t * W:(t + 1) * W],
                        in_=red_src[:].rearrange("p (l n) -> p l n", n=NST),
                        axis=AX.X, op=ALU.add)
                    inc(ve.scalar_tensor_tensor(
                        out=S_y[:, t * W:(t + 1) * W],
                        in0=S_usl[:, t * W:(t + 1) * W],
                        scalar=sbD[:, t:t + 1],
                        in1=S_y[:, t * W:(t + 1) * W],
                        op0=ALU.mult, op1=ALU.add), "dve")
                    stok[("red", idx)] = cnt["dve"]
                    if t == NT - 1:
                        stok[("stt_done", w)] = cnt["dve"]
                sec("vector", f_red)

            def f_ys(sy, w=w):
                wt(sy, "sy", "dve", stok[("stt_done", w)])
                o = bass.AP(y_in, (w // 4) * (S * T) + (w % 4) * W,
                            [[T, 128], [128 * T, 4], [1, W]])
                inc(sy.dma_start(out=o, in_=S_y[:]), "sy", 16)
                stok[("ystore", w)] = cnt["sy"]
            sec("sync", f_ys)


        dn = {}

        def down_chunk_head(ci):
            pr, kh = divmod(ci, 4)
            lts = [2 * pr, 2 * pr + 1]

            def f_gld(po, ci=ci, pr=pr, kh=kh, lts=lts):
                wt(po, "po", "gd", gu_tok[("gst_store", kh * 8 + 7)])
                wt(po, "po", "pe", gu_tok[("umm", 31)])
                for i, lt in enumerate(lts):
                    wt(po, "po", "dve", dn.get(("prod", ci - 1, i), 0))
                    ag = bass.AP(g_buf, kh * 16 * 128 * T + lt * 128,
                                 [[T, 128], [128 * T, 16], [1, 128]])
                    au = bass.AP(u_buf, kh * 16 * 128 * T + lt * 128,
                                 [[T, 128], [128 * T, 16], [1, 128]])
                    inc(po.dma_start(out=Fg_ld[i][:], in_=ag), "gd", 16)
                    inc(po.dma_start(out=Fu_ld[i][:], in_=au), "gd", 16)
                    wt(po, "po", "gd", cnt["gd"])
                    dn[("gld", ci, i)] = cnt["gd"]
            sec("gpsimd", f_gld)

            def f_prod(ve, ci=ci, pr=pr, kh=kh):
                wt(ve, "ve", "gd", gu_tok[("gst_store", 31)])
                prev = dn.get(("mmlast",) + divmod(ci - 1, 4), 0)
                wt(ve, "ve", "pe", prev)
                for i in range(2):
                    wt(ve, "ve", "gd", dn[("gld", ci, i)])
                    inc(ve.tensor_mul(F_glt[i][:], Fg_ld[i][:],
                                      Fu_ld[i][:]), "dve")
                    dn[("prod", ci, i)] = cnt["dve"]
            sec("vector", f_prod)

        def down_chunk_tail(ci):
            pr, kh = divmod(ci, 4)
            for k in range(16):
                kk = kh * 16 + k
                spar = kk % 2

                def f_sl(sc, ci=ci, pr=pr, kh=kh, k=k, kk=kk, spar=spar):
                    wt(sc, "sc", "gd", tok["repack_wdn"])
                    if ci == 0 and k < 2:
                        wt(sc, "sc", "gd", gu_tok[("gst_store", 31)])
                    wt(sc, "sc", "pe", dn.get(("kmm", pr, kh, k - 2), 0))
                    if k < 2:
                        wt(sc, "sc", "pe",
                           dn.get(("kmm", pr, kh - 1, 14 + k),
                                  dn.get(("kmm", pr - 1, 3, 14 + k), 0)))
                    inc(sc.dma_start(
                        out=F_dstr[spar][:],
                        in_=wdnTc.ap()[kk * 128:(kk + 1) * 128, :]),
                        "ad", 16)
                    dn[("str", pr, kh, k)] = cnt["ad"]
                sec("scalar", f_sl)

                def f_km(te, ci=ci, pr=pr, kh=kh, k=k, kk=kk, spar=spar):
                    for i in range(2):
                        wt(te, "te", "dve", dn[("prod", ci, i)])
                    wt(te, "te", "ad", dn[("str", pr, kh, k)])
                    if kk == 0:
                        for b in range(8):
                            bank_wait(te, "te", b)
                    m = None
                    for i in range(2):
                        lhs = F_glt[i][:, k * 128:(k + 1) * 128]
                        for c in range(4):
                            m = te.matmul(
                                P[4 * i + c][:], lhsT=lhs,
                                rhs=F_dstr[spar][:, c * 512:(c + 1) * 512],
                                start=(kk == 0), stop=(kk == 63))
                    inc(m, "pe")
                    dn[("kmm", pr, kh, k)] = cnt["pe"]
                sec("tensor", f_km)
            dn[("mmlast", pr, kh)] = cnt["pe"]

        def down_evict(pr):
            lts = [2 * pr, 2 * pr + 1]
            for i, lt in enumerate(lts):
                for c in range(4):
                    def f_ev(sc, pr=pr, i=i, c=c):
                        wt(sc, "sc", "pe", dn[("mmlast", pr, 3)])
                        jp = 4 * i + c - 2
                        wk = (("fstore", pr, jp // 4, jp % 4) if jp >= 0
                              else ("fstore", pr - 1, (jp + 8) // 4,
                                    (jp + 8) % 4))
                        wt(sc, "sc", "sy", dn.get(wk, 0))
                        v = inc(sc.activation(out=F_ffn[c % 2][:],
                                              in_=P[4 * i + c][:],
                                              func=AF.Copy), "act")
                        bank[4 * i + c] = ("act", v)
                        dn[("fev", pr, i, c)] = cnt["act"]
                    sec("scalar", f_ev)

                    def f_fs(sy, pr=pr, i=i, c=c, lt=lt):
                        wt(sy, "sy", "act", dn[("fev", pr, i, c)])
                        inc(sy.dma_start(
                            out=ffn_buf.ap()[lt * 128:(lt + 1) * 128,
                                             c * 512:(c + 1) * 512],
                            in_=F_ffn[c % 2][:]), "sy", 16)
                        dn[("fstore", pr, i, c)] = cnt["sy"]
                    sec("sync", f_fs)
            if pr == 3:
                tok["ffn_done"] = cnt["sy"]

        DS = 16
        for w in range(NW):
            ci = w - DS
            if w == 16:
                gu_ms(30)
                gu_ms(31)
            if 0 <= ci < 16:
                down_chunk_head(ci)
            scan_w(w)
            if 1 <= w <= 15:
                gu_ms(2 * (w - 1))
                gu_ms(2 * w - 1)
            if 0 <= w <= 10:
                def f_cast_late(po, w=w):
                    wt(po, "po", "cc", tok["cc_du"])
                    if w >= 1:
                        wt(po, "po", "dve", stok[("stt_done", w - 1)])
                    if w < 3:
                        ci2 = w + 1
                        rs = slice(ci2 * 2048, (ci2 + 1) * 2048)
                        inc(po.dma_start(out=wgTb.ap()[:, rs],
                                         in_=wgT[:, rs]), "gd", 16)
                        tok[("staged", "wg", ci2)] = cnt["gd"]
                        wt(po, "po", "gd", cnt["gd"])
                        inc(po.dma_start(out=wuTb.ap()[:, rs],
                                         in_=wuT[:, rs]), "gd", 16)
                        tok[("staged", "wu", ci2)] = cnt["gd"]
                        wt(po, "po", "gd", cnt["gd"])
                    elif w < 7:
                        pc = w - 3
                        rs = slice(pc * 2048, (pc + 1) * 2048)
                        inc(po.dma_start(out=wdnTc.ap()[rs, :],
                                         in_=wdnT[rs, :]), "gd", 16)
                        wt(po, "po", "gd", cnt["gd"])
                        if pc == 3:
                            tok["repack_wdn"] = cnt["gd"]
                    else:
                        pc = w - 7
                        rs = slice(pc * 1024, (pc + 1) * 1024)
                        inc(po.dma_start(out=woutTc.ap()[rs, :],
                                         in_=woutT[rs, :]), "gd", 16)
                        wt(po, "po", "gd", cnt["gd"])
                        if pc == 10 - 7:
                            tok["repack_wout"] = cnt["gd"]
                sec("gpsimd", f_cast_late)
            if 0 <= ci < 16:
                down_chunk_tail(ci)
            if w >= DS + 3 and (w - DS - 3) % 4 == 0 and (w - DS - 3) // 4 < 4:
                down_evict((w - DS - 3) // 4)

        def f_warm(te):
            for b in range(8):
                bank_wait(te, "te", b)
            m = None
            for _ in range(800):
                m = te.matmul(P[7][:], lhsT=P5_wo[0][:, 0:128],
                              rhs=P5_wo[1][:, 0:512], start=True, stop=True)
            inc(m, "pe")
            tok["warm"] = cnt["pe"]
        sec("tensor", f_warm)

        def f_agy(po):
            wt(po, "po", "sy", stok[("ystore", NW - 1)])
            wt(po, "po", "sy", tok["ffn_done"])
            wt(po, "po", "pe", dn[("mmlast", 3, 3)])
            m = po.collective_compute("AllToAll", ALU.bypass,
                                      replica_groups=RG,
                                      ins=[y_in.ap()], outs=[y_loc.ap()])
            inc(m, "cc")
            wt(po, "po", "cc", cnt["cc"])
            m = po.collective_compute("AllGather", ALU.bypass,
                                      replica_groups=RG,
                                      ins=[fence_in.ap()],
                                      outs=[fence_out[1].ap()])
            inc(m, "cc")
            tok["cc_y"] = cnt["cc"]
        sec("gpsimd", f_agy)

        p5 = {}
        for k in range(32):
            zr = k % 2

            def f_sy(sy, k=k, zr=zr):
                wt(sy, "sy", "cc", tok["cc_y"])
                wt(sy, "sy", "dve", p5.get(("y2", k - 2), 0))
                inc(sy.dma_start(out=P5_zs[zr][:],
                                 in_=z_buf.ap()[k * 128:(k + 1) * 128, :]),
                    "sy", 16)
                p5[("z", k)] = cnt["sy"]
                inc(sy.dma_start(out=P5_y[zr][:],
                                 in_=y_loc.ap()[k * 128:(k + 1) * 128, :]),
                    "sy", 16)
                p5[("y", k)] = cnt["sy"]
            sec("sync", f_sy)

            def f_sc(sc, k=k, zr=zr):
                wt(sc, "sc", "sy", p5[("z", k)])
                wt(sc, "sc", "dve", p5.get(("y2", k - 2), 0))
                inc(sc.activation(out=P5_zs[zr][:], in_=P5_zs[zr][:],
                                  func=AF.Silu), "act")
                p5[("silu", k)] = cnt["act"]
            sec("scalar", f_sc)

            def f_ve(ve, k=k, zr=zr):
                wt(ve, "ve", "act", p5[("silu", k)])
                wt(ve, "ve", "sy", p5[("y", k)])
                inc(ve.tensor_mul(P5_y2[:, k * 1024:(k + 1) * 1024],
                                  P5_y[zr][:], P5_zs[zr][:]), "dve")
                p5[("y2", k)] = cnt["dve"]
            sec("vector", f_ve)
        y2_done = cnt["dve"]

        def f_brow(sy):
            wt(sy, "sy", "cc", tok["cc_y"])
            inc(sy.dma_start(out=P5_hb[0:1, 0:2048], in_=bout), "sy", 16)
            tok["bout_ld"] = cnt["sy"]
        sec("sync", f_brow)

        def f_brep_te(te):
            wt(te, "te", "sy", tok["bout_ld"])
            for b in range(4):
                bank_wait(te, "te", b)
            m = None
            for c in range(4):
                m = te.matmul(P[c][:], lhsT=sbONES[:],
                              rhs=P5_hb[0:1, c * 512:(c + 1) * 512],
                              start=True, stop=True)
            inc(m, "pe")
            tok["brep_mm"] = cnt["pe"]
        sec("tensor", f_brep_te)

        def f_brep_sc(sc):
            wt(sc, "sc", "pe", tok["brep_mm"])
            for c in range(4):
                v = inc(sc.activation(out=P5_brep[:, c * 512:(c + 1) * 512],
                                      in_=P[c][:], func=AF.Copy), "act")
                bank[c] = ("act", v)
            tok["brep"] = cnt["act"]
        sec("scalar", f_brep_sc)

        for pr in range(4):
            lts = [2 * pr, 2 * pr + 1]
            for k in range(32):
                spar = k % 2

                def f_wl(sy, pr=pr, k=k, spar=spar):
                    wt(sy, "sy", "gd", tok["repack_wout"])
                    wt(sy, "sy", "pe", p5.get(("womm", pr, k - 2),
                                              p5.get(("womm", pr - 1, 30 + k),
                                                     0)))
                    inc(sy.dma_start(
                        out=P5_wo[spar][:],
                        in_=woutTc.ap()[k * 128:(k + 1) * 128, :]), "sy", 16)
                    p5[("wo", pr, k)] = cnt["sy"]
                sec("sync", f_wl)

                def f_wm(te, pr=pr, k=k, spar=spar, lts=lts):
                    wt(te, "te", "dve", y2_done)
                    wt(te, "te", "sy", p5[("wo", pr, k)])
                    if k == 0:
                        for b in range(8):
                            bank_wait(te, "te", b)
                    m = None
                    for i in range(2):
                        lhs = P5_y2[:, k * 1024 + lts[i] * 128:
                                    k * 1024 + (lts[i] + 1) * 128]
                        for c in range(4):
                            m = te.matmul(
                                P[4 * i + c][:], lhsT=lhs,
                                rhs=P5_wo[spar][:, c * 512:(c + 1) * 512],
                                start=(k == 0), stop=(k == 31))
                    inc(m, "pe")
                    p5[("womm", pr, k)] = cnt["pe"]
                sec("tensor", f_wm)
            for i in range(2):
                p5[("wmm", pr, i)] = cnt["pe"]

            for i, lt in enumerate(lts):
                def f_ld(sy, pr=pr, i=i, lt=lt):
                    wt(sy, "sy", "sy", tok["ffn_done"])
                    wt(sy, "sy", "pe", tok["brep_mm"])
                    if i == 0:
                        wt(sy, "sy", "dve", p5.get(("comb", pr - 1, 1), 0))
                    else:
                        wt(sy, "sy", "dve", p5[("comb", pr, 0)])
                    inc(sy.dma_start(
                        out=P5_ffn[:],
                        in_=ffn_buf.ap()[lt * 128:(lt + 1) * 128, :]),
                        "sy", 16)
                    p5[("ffl", pr, i)] = cnt["sy"]
                    inc(sy.dma_start(out=P5_hb[:],
                                     in_=h_main[lt * 128:(lt + 1) * 128, :]),
                        "sy", 16)
                    p5[("hl", pr, i)] = cnt["sy"]
                sec("sync", f_ld)

                def f_cb(ve, pr=pr, i=i):
                    orr = i % 2
                    wt(ve, "ve", "pe", p5[("wmm", pr, i)])
                    wt(ve, "ve", "sy", p5[("hl", pr, i)])
                    wt(ve, "ve", "act", tok["brep"])
                    wt(ve, "ve", "sy", p5.get(("ostore", pr - 1, i), 0))
                    ve.tensor_add(P5_hb[:], P5_hb[:], P5_brep[:])
                    for c in range(4):
                        ve.tensor_add(P5_out[orr][:, c * 512:(c + 1) * 512],
                                      P[4 * i + c][:],
                                      P5_ffn[:, c * 512:(c + 1) * 512])
                    v = inc(ve.tensor_add(P5_out[orr][:], P5_out[orr][:],
                                          P5_hb[:]), "dve")
                    for c in range(4):
                        bank[4 * i + c] = ("dve", v)
                    p5[("comb", pr, i)] = cnt["dve"]
                sec("vector", f_cb)

                def f_os(sy, pr=pr, i=i, lt=lt):
                    orr = i % 2
                    wt(sy, "sy", "dve", p5[("comb", pr, i)])
                    inc(sy.dma_start(out=out_o[lt * 128:(lt + 1) * 128, :],
                                     in_=P5_out[orr][:]), "sy", 16)
                    p5[("ostore", pr, i)] = cnt["sy"]
                sec("sync", f_os)

        def f_final(sy):
            sy.wait_ge(sems["sy"], cnt["sy"])
        sec("sync", f_final)

    return nc


_NC_CACHE = {}


def kernel(L0_attn_hidden_states, L0_attn_W_in, L0_attn_conv_w, L0_attn_conv_b,
           L0_attn_W_x, L0_attn_W_dt, L0_attn_b_dt, L0_attn_A_log, L0_attn_D,
           L0_attn_W_out, L0_attn_b_out, L0_ffn_w_gate, L0_ffn_w_up,
           L0_ffn_w_down):
    h = np.ascontiguousarray(np.asarray(L0_attn_hidden_states, np.float32)
                             .reshape(8192, 2048))
    Win = np.ascontiguousarray(np.asarray(L0_attn_W_in, np.float32))
    Wx = np.ascontiguousarray(np.asarray(L0_attn_W_x, np.float32))
    Wdt = np.ascontiguousarray(np.asarray(L0_attn_W_dt, np.float32))
    Wout = np.ascontiguousarray(np.asarray(L0_attn_W_out, np.float32))
    Wg = np.ascontiguousarray(np.asarray(L0_ffn_w_gate, np.float32))
    Wu = np.ascontiguousarray(np.asarray(L0_ffn_w_up, np.float32))
    Wd = np.ascontiguousarray(np.asarray(L0_ffn_w_down, np.float32))
    cw = np.asarray(L0_attn_conv_w, np.float32)
    cb = np.asarray(L0_attn_conv_b, np.float32)
    bdt_ = np.asarray(L0_attn_b_dt, np.float32)
    Alog = np.asarray(L0_attn_A_log, np.float32)
    Dv = np.asarray(L0_attn_D, np.float32)
    bo = np.asarray(L0_attn_b_out, np.float32)

    wx_h = np.ascontiguousarray(
        Wx.T.reshape(32, 128, 160).transpose(1, 0, 2).reshape(128, 5120))
    wdt_h = np.ascontiguousarray(Wdt.T)
    winT_h = np.ascontiguousarray(Win.T)
    wgT_h = np.ascontiguousarray(Wg.T)
    wuT_h = np.ascontiguousarray(Wu.T)
    wdnT_h = np.ascontiguousarray(Wd.T)
    woutT_h = np.ascontiguousarray(Wout.T)
    convw_h = np.ascontiguousarray(
        cw.reshape(32, 128, 4).transpose(1, 0, 2).reshape(128, 128))
    convb_h = np.ascontiguousarray(cb.reshape(32, 128).T)
    bdt_h = np.ascontiguousarray(bdt_.reshape(32, 128).T)
    ones_h = np.ones((1, 128), np.float32)
    id_h = np.eye(128, dtype=np.float32)
    bout_h = np.ascontiguousarray(bo.reshape(1, 2048))

    in_maps = []
    for c in range(NCORES):
        halo = (np.zeros((3, 2048), np.float32) if c == 0
                else np.ascontiguousarray(h[c * 1024 - 3:c * 1024]))
        asl = np.ascontiguousarray(
            Alog[c * 512:(c + 1) * 512].reshape(4, 128, 16)
            .transpose(1, 0, 2).reshape(128, 64))
        dsl = np.ascontiguousarray(Dv[c * 512:(c + 1) * 512].reshape(4, 128).T)
        in_maps.append(dict(
            h_main=np.ascontiguousarray(h[c * 1024:(c + 1) * 1024]),
            h_halo=halo,
            winT=winT_h, woutT=woutT_h, wgT=wgT_h, wuT=wuT_h,
            wdnT=wdnT_h, wx_in=wx_h, wdt_in=wdt_h,
            a_sl=asl, d_sl=dsl, convw=convw_h, convb=convb_h, bdt=bdt_h,
            bout=bout_h, ones1=ones_h, ident=id_h,
        ))

    if "nc" not in _NC_CACHE:
        _NC_CACHE["nc"] = build_program()
    nc = _NC_CACHE["nc"]
    res = run_bass_kernel_spmd(nc, in_maps, core_ids=list(range(NCORES)))
    out = np.concatenate([res.results[c]["o"] for c in range(NCORES)], axis=0)
    return out.reshape(1, 8192, 2048).astype(np.float32)
